# version 20
# baseline (speedup 1.0000x reference)
import contextlib

import numpy as np
import ml_dtypes

import concourse.bass as bass
import concourse.mybir as mybir
from concourse.bass_utils import run_bass_kernel_spmd

F32 = mybir.dt.float32
BF16 = mybir.dt.bfloat16
AF = mybir.ActivationFunctionType
ALU = mybir.AluOpType
AX = mybir.AxisListType

MIX = 0.05
C1 = 2.0 * (1.0 + MIX)
EPS = 1e-5
D = 1024
H = 16
HD = 64
T = 2048
OD = 8
KT = 8
TC = 4
NCH = 8
CT = 256
NCORE = 8


def build_nc():
    nc = bass.Bass("TRN2", target_bir_lowering=False, detect_race_conditions=False)

    xT = nc.dram_tensor("xT", [D, T], F32, kind="ExternalInput")
    wqT = nc.dram_tensor("wqT", [D, D], F32, kind="ExternalInput")
    wkT = nc.dram_tensor("wkT", [D, D], F32, kind="ExternalInput")
    wvT = nc.dram_tensor("wvT", [D, D], F32, kind="ExternalInput")
    wgT = nc.dram_tensor("wgT", [D, D], F32, kind="ExternalInput")
    woT = nc.dram_tensor("woT", [D, D], F32, kind="ExternalInput")
    wfT = nc.dram_tensor("wfT", [D, D], F32, kind="ExternalInput")
    wbfsT = nc.dram_tensor("wbfsT", [D, 48], F32, kind="ExternalInput")
    bias48 = nc.dram_tensor("bias48", [48, 1], F32, kind="ExternalInput")
    ib16 = nc.dram_tensor("ib16", [16, D], BF16, kind="ExternalInput")
    sel16 = nc.dram_tensor("sel16", [16, 8], F32, kind="ExternalInput")
    cst = nc.dram_tensor("cst", [1, 4], F32, kind="ExternalInput")
    out = nc.dram_tensor("out", [D, T], F32, kind="ExternalOutput")

    u_scr = nc.dram_tensor("u_scr", [D * T], BF16)
    zl_scr = nc.dram_tensor("zl_scr", [D * T], F32)
    qg_scr = nc.dram_tensor("qg_scr", [D * T], BF16)
    coef_scr = nc.dram_tensor("coef_scr", [2 * H * T], F32)
    wch_scr = nc.dram_tensor("wch_scr", [128 * 128], F32)
    rch_scr = nc.dram_tensor("rch_scr", [128 * 4], F32)
    corrl_scr = nc.dram_tensor("corrl_scr", [NCH * 32 * D], F32)
    rrow_scr = nc.dram_tensor("rrow_scr", [32 * T], F32)
    agin_scr = nc.dram_tensor("agin_scr", [1, 2112], F32)
    agout_scr = nc.dram_tensor("agout_scr", [8, 2112], F32)

    nT = nV = nS = nG = 0
    ndW = ndX = ndA = ndB = ndC = ndO = ncc = 0

    with contextlib.ExitStack() as _st:
        E = _st.enter_context
        ones_p1 = E(nc.sbuf_tensor("ones_p1", [128, 1], BF16))
        ones_p1f = E(nc.sbuf_tensor("ones_p1f", [128, 1], F32))
        ones_1r = E(nc.sbuf_tensor("ones_1r", [1, 128], F32))
        ones_1r16 = E(nc.sbuf_tensor("ones_1r16", [1, 128], BF16))
        sel_sb = E(nc.sbuf_tensor("sel_sb", [16, 8], F32))
        rsum16 = E(nc.sbuf_tensor("rsum16", [128, 2 * KT], BF16))
        rsumf = E(nc.sbuf_tensor("rsumf", [128, KT], F32))
        cst_sb = E(nc.sbuf_tensor("cst_sb", [1, 4], F32))
        zro_sb = E(nc.sbuf_tensor("zro_sb", [128, 128], F32))
        pb = E(nc.psum_tensor("pb", [128, 4096], F32))
        dW = E(nc.semaphore("dW"))
        dX = E(nc.semaphore("dX"))
        dA = E(nc.semaphore("dA"))
        dB = E(nc.semaphore("dB"))
        dCa = E(nc.semaphore("dCa"))
        dCb = E(nc.semaphore("dCb"))
        dCc = E(nc.semaphore("dCc"))
        dO = E(nc.semaphore("dO"))
        sT = E(nc.semaphore("sT"))
        sV = E(nc.semaphore("sV"))
        sS = E(nc.semaphore("sS"))
        cc = E(nc.semaphore("cc"))
        sG = E(nc.semaphore("sG"))
        dG = E(nc.semaphore("dG"))
        wA = E(nc.semaphore("wA"))
        wB = E(nc.semaphore("wB"))
        dCd = E(nc.semaphore("dCd"))
        dCe = E(nc.semaphore("dCe"))
        blk = E(nc.Block("ALL"))
        with contextlib.ExitStack() as _sa:
            EA = _sa.enter_context
            xl_sb = EA(nc.sbuf_tensor("xl_sb", [128, KT * T], BF16))
            ev_a = EA(nc.sbuf_tensor("ev_a", [128, 2 * 512], BF16, side="right"))
            ev_b = EA(nc.sbuf_tensor("ev_b", [128, 2 * 512], BF16, side="right"))
            negs16 = EA(nc.sbuf_tensor("negs16", [1, T], BF16, side="right"))
            cs_q = EA(nc.sbuf_tensor("cs_q", [1, D], BF16, side="right"))
            cs_g = EA(nc.sbuf_tensor("cs_g", [1, D], BF16, side="right"))
            qgt_sb = EA(nc.sbuf_tensor("qgt_sb", [128, 2 * 512], BF16, side="right"))
            _sa2 = contextlib.ExitStack()
            EA2 = _sa2.enter_context
            ib_sb = EA2(nc.sbuf_tensor("ib_sb", [16, D], BF16))
            rb_sb = EA2(nc.sbuf_tensor("rb_sb", [128, T], F32))
            srow_a = EA2(nc.sbuf_tensor("srow_a", [1, T], F32))[0:1, :]
            srow_b = EA2(nc.sbuf_tensor("srow_b", [1, T], F32))[0:1, :]
            srow_c = EA2(nc.sbuf_tensor("srow_c", [1, T], F32))[0:1, :]
            cs_k = EA2(nc.sbuf_tensor("cs_k", [1, D], BF16))
            cs_v = EA2(nc.sbuf_tensor("cs_v", [1, D], BF16))
            cs_b = EA2(nc.sbuf_tensor("cs_b", [1, 48], BF16))
            wbfs_sb = EA2(nc.sbuf_tensor("wbfs_sb", [128, KT * 48], BF16))
            b48_sb = EA2(nc.sbuf_tensor("b48_sb", [48, 1], F32))
            _sx = contextlib.ExitStack()
            x_sb = _sx.enter_context(nc.sbuf_tensor("x_sb", [128, OD * T], F32))
            xsq16 = _sx.enter_context(nc.sbuf_tensor("xsq16", [128, T], BF16))
            T0, V0, S0 = nT, nV, nS
            if True:
                @blk.sync
                def _(sync):
                    for od in range(OD):
                        sync.dma_start(x_sb[:, od * T:(od + 1) * T],
                                       xT[od * 128:(od + 1) * 128, :]).then_inc(dX, 16)

                ndX += 16 * OD

                @blk.gpsimd
                def _(g):
                    g.dma_start(cst_sb[:], cst[:]).then_inc(dW, 16)
                    g.memset(ones_p1[:], 1.0)
                    g.memset(ones_p1f[:], 1.0)
                    g.memset(ones_1r[:], 1.0)
                    g.memset(zro_sb[:], 0.0)
                    g.memset(ones_1r16[:], 1.0).then_inc(sG, 1)
                    g.dma_start(ib_sb[:], ib16[:]).then_inc(dW, 16)
                    g.dma_start(sel_sb[:], sel16[:]).then_inc(dW, 16)
                    g.dma_start(b48_sb[:], bias48[:]).then_inc(dW, 16)
                    for kt in range(KT):
                        g.dma_start(wbfs_sb[:, kt * 48:(kt + 1) * 48],
                                    wbfsT[kt * 128:(kt + 1) * 128, :]).then_inc(dW, 16)

                ndW += 16 * (4 + KT)
                nG += 1

                @blk.scalar
                def _(s):
                    s.wait_ge(dX, 16 * OD)
                    for kt in range(KT):
                        if kt >= 1:
                            s.wait_ge(sT, T0 + kt)
                        s.activation(xsq16[:, 0:T],
                                     x_sb[:, kt * T:(kt + 1) * T],
                                     AF.Square).then_inc(sS, 1)

                nS += KT

                @blk.tensor
                def _(t):
                    t.wait_ge(dX, 16 * OD)
                    t.wait_ge(sG, 1)
                    for kt in range(KT):
                        t.wait_ge(sS, S0 + kt + 1)
                        for tc in range(TC):
                            t.matmul(pb[0:1, tc * 512:(tc + 1) * 512], ones_p1f[:],
                                     x_sb[:, kt * T + tc * 512:kt * T + (tc + 1) * 512],
                                     start=(kt == 0), stop=(kt == KT - 1))
                        for tc in range(TC):
                            last = (kt == KT - 1) and (tc == TC - 1)
                            mm = t.matmul(
                                pb[0:1, 2048 + tc * 512:2048 + (tc + 1) * 512], ones_p1[:],
                                xsq16[:, tc * 512:(tc + 1) * 512],
                                start=(kt == 0), stop=(kt == KT - 1))
                        mm.then_inc(sT, 1)

                nT += KT

                @blk.vector
                def _(v):
                    v.wait_ge(sT, T0 + KT)
                    v.wait_ge(sG, 1)
                    v.tensor_scalar(srow_a, pb[0:1, 0:2048], 1.0 / D, None, ALU.mult)
                    v.tensor_scalar(srow_b, pb[0:1, 2048:4096], 1.0 / D, None, ALU.mult)
                    v.tensor_mul(srow_c, srow_a, srow_a)
                    v.tensor_sub(srow_b, srow_b, srow_c).then_inc(sV, 1)

                nV += 1

                @blk.scalar
                def _(s):
                    s.wait_ge(sV, V0 + 1)
                    s.wait_ge(dW, 16 * (4 + KT))
                    s.activation(srow_c, srow_b, AF.Ln, bias=cst_sb[0:1, 0:1])
                    s.activation(srow_c, srow_c, AF.Exp, scale=cst_sb[0:1, 1:2]).then_inc(sS, 1)

                nS += 1

                @blk.vector
                def _(v):
                    v.wait_ge(sS, S0 + KT + 1)
                    v.scalar_tensor_tensor(negs16[:], srow_a, -1.0, srow_c,
                                           ALU.mult, ALU.mult).then_inc(sV, 1)

                nV += 1

                @blk.tensor
                def _(t):
                    t.wait_ge(sV, V0 + 2)
                    for tc in range(TC):
                        mm = t.matmul(pb[:, 2048 + tc * 512:2048 + (tc + 1) * 512],
                                      ones_1r[:], srow_c[:, tc * 512:(tc + 1) * 512],
                                      start=True, stop=True)
                    mm.then_inc(sT, 1)

                nT += 1

                @blk.vector
                def _(v):
                    v.wait_ge(sT, T0 + KT + 1)
                    for tc in range(TC):
                        v.tensor_copy(rb_sb[:, tc * 512:(tc + 1) * 512],
                                      pb[:, 2048 + tc * 512:2048 + (tc + 1) * 512])
                    for od in range(OD):
                        vv = v.tensor_mul(xl_sb[:, od * T:(od + 1) * T],
                                          x_sb[:, od * T:(od + 1) * T], rb_sb[:])
                    vv.then_inc(sV, 1)

                nV += 1

            _sx.close()
            wp_sb = EA(nc.sbuf_tensor("wp_sb", [128, 2 * KT * D], BF16, side="right"))
            bfs_sb = EA2(nc.sbuf_tensor("bfs_sb", [48, T], F32))
            f_sb = EA2(nc.sbuf_tensor("f_sb", [16, T], F32))
            s_sb = EA2(nc.sbuf_tensor("s_sb", [16, T], F32))
            brow16 = EA2(nc.sbuf_tensor("brow16", [16, T], BF16))
            trow = EA2(nc.sbuf_tensor("trow", [16, T], F32))
            kv_t = EA2(nc.sbuf_tensor("kv_t", [128, 2 * 512], F32))
            u_t = EA2(nc.sbuf_tensor("u_t", [128, 2 * 512], BF16))

            wstg = nc.alloc_sbuf_tensor_at("wstg", [128, 2 * D], F32, offset=160512)
            cast_S = nS
            if True:
                @blk.sync
                def _(sync):
                    for i in range(2 * KT):
                        kt, is_v = i // 2, i % 2
                        if i >= 2:
                            sync.wait_ge(sS, cast_S + i - 1)
                        sync.dma_start(wstg[:, (i % 2) * D:(i % 2 + 1) * D],
                                       (wvT if is_v else wkT)[kt * 128:(kt + 1) * 128, :]
                                       ).then_inc(wA if i % 2 == 0 else wB, 16)

                @blk.scalar
                def _(s):
                    for i in range(2 * KT):
                        kt, is_v = i // 2, i % 2
                        s.wait_ge(wA if i % 2 == 0 else wB, 16 * (i // 2 + 1))
                        s.activation(wp_sb[:, (KT * is_v + kt) * D:(KT * is_v + kt + 1) * D],
                                     wstg[:, (i % 2) * D:(i % 2 + 1) * D],
                                     AF.Copy).then_inc(sS, 1)

            nwA = nwB = 16 * KT
            nS += 2 * KT

            T1, V1, S1 = nT, nV, nS
            if True:
                @blk.tensor
                def _(t):
                    t.wait_ge(sS, cast_S + 2 * KT)
                    t.wait_ge(sV, V1)
                    for kt in range(KT):
                        for nh in range(2):
                            t.matmul(pb[0:1, nh * 512:(nh + 1) * 512], ones_p1[:],
                                     wp_sb[:, kt * D + nh * 512:kt * D + (nh + 1) * 512],
                                     start=(kt == 0), stop=(kt == KT - 1))
                            t.matmul(pb[0:1, 1024 + nh * 512:1536 + nh * 512], ones_p1[:],
                                     wp_sb[:, (KT + kt) * D + nh * 512:(KT + kt) * D + (nh + 1) * 512],
                                     start=(kt == 0), stop=(kt == KT - 1))
                        mm = t.matmul(pb[0:1, 2048:2096], ones_p1[:],
                                      wbfs_sb[:, kt * 48:(kt + 1) * 48],
                                      start=(kt == 0), stop=(kt == KT - 1))
                    mm.then_inc(sT, 1)

                nT += 1

                @blk.vector
                def _(v):
                    v.wait_ge(sT, T1 + 1)
                    v.tensor_copy(cs_k[:], pb[0:1, 0:1024])
                    v.tensor_copy(cs_v[:], pb[0:1, 1024:2048])
                    v.tensor_copy(cs_b[:], pb[0:1, 2048:2096]).then_inc(sV, 1)

                nV += 1

                @blk.tensor
                def _(t):
                    t.wait_ge(sV, V1 + 1)
                    for kt in range(KT):
                        for tc in range(TC):
                            t.matmul(pb[0:48, tc * 512:(tc + 1) * 512],
                                     wbfs_sb[:, kt * 48:(kt + 1) * 48],
                                     xl_sb[:, kt * T + tc * 512:kt * T + (tc + 1) * 512],
                                     start=(kt == 0), stop=False)
                    for tc in range(TC):
                        t.matmul(pb[0:48, tc * 512:(tc + 1) * 512], cs_b[:],
                                 negs16[0:1, tc * 512:(tc + 1) * 512],
                                 start=False, stop=True).then_inc(sT, 1)

                nT += TC

                @blk.scalar
                def _(s):
                    for tc in range(TC):
                        s.wait_ge(sT, T1 + 1 + tc + 1)
                        s.activation(bfs_sb[0:48, tc * 512:(tc + 1) * 512],
                                     pb[0:48, tc * 512:(tc + 1) * 512],
                                     AF.Sigmoid, bias=b48_sb[:]).then_inc(sS, 1)

                nS += TC

                @blk.sync
                def _(sync):
                    sync.wait_ge(sS, S1 + TC)
                    sync.dma_start(f_sb[:], bfs_sb[16:32, :]).then_inc(dA, 16)
                    sync.dma_start(s_sb[:], bfs_sb[32:48, :]).then_inc(dA, 16)

                @blk.vector
                def _(v):
                    v.wait_ge(dA, ndA + 32)
                    v.tensor_copy(brow16[:], bfs_sb[0:16, :])
                    v.tensor_add(trow[:], f_sb[:], s_sb[:])
                    v.tensor_sub(s_sb[:], f_sb[:], s_sb[:]).then_inc(sV, 1)

                nV += 1

                @blk.sync
                def _(sync):
                    sync.wait_ge(sV, V1 + 2)
                    sync.dma_start(bass.AP(coef_scr, 0, [[T, 16], [1, T]]),
                                   trow[:]).then_inc(dA, 16)
                    sync.dma_start(bass.AP(coef_scr, H * T, [[T, 16], [1, T]]),
                                   s_sb[:]).then_inc(dA, 16)

                ndA += 64

            T2, V2, S2, A2 = nT, nV, nS, ndA
            if True:
                @blk.tensor
                def _(t):
                    t.wait_ge(sV, V2)
                    for od in range(OD):
                        if od > 0:
                            t.wait_ge(sV, V2 + od * 8)
                        for kt in range(KT):
                            for tc in range(TC):
                                t.matmul(pb[:, tc * 512:(tc + 1) * 512],
                                         wp_sb[:, kt * D + od * 128:kt * D + od * 128 + 128],
                                         xl_sb[:, kt * T + tc * 512:kt * T + (tc + 1) * 512],
                                         start=(kt == 0), stop=False)
                            for tc in range(TC):
                                t.matmul(pb[:, 2048 + tc * 512:2048 + (tc + 1) * 512],
                                         wp_sb[:, (KT + kt) * D + od * 128:(KT + kt) * D + od * 128 + 128],
                                         xl_sb[:, kt * T + tc * 512:kt * T + (tc + 1) * 512],
                                         start=(kt == 0), stop=False)
                        for tc in range(TC):
                            t.matmul(pb[:, tc * 512:(tc + 1) * 512],
                                     cs_k[0:1, od * 128:(od + 1) * 128],
                                     negs16[0:1, tc * 512:(tc + 1) * 512],
                                     start=False, stop=True)
                            t.matmul(pb[:, 2048 + tc * 512:2048 + (tc + 1) * 512],
                                     cs_v[0:1, od * 128:(od + 1) * 128],
                                     negs16[0:1, tc * 512:(tc + 1) * 512],
                                     start=False, stop=True).then_inc(sT, 1)
                        for tc in range(TC):
                            t.wait_ge(sS, S2 + od * 8 + 2 * (tc + 1))
                            t.matmul(pb[:, tc * 512:(tc + 1) * 512],
                                     ib_sb[:, od * 128:(od + 1) * 128],
                                     brow16[0:16, tc * 512:(tc + 1) * 512],
                                     start=True, stop=True).then_inc(sT, 1)

                nT += OD * 8

                @blk.scalar
                def _(s):
                    for od in range(OD):
                        for tc in range(TC):
                            s.wait_ge(sT, T2 + od * 8 + tc + 1)
                            if tc >= 2:
                                s.wait_ge(sV, V2 + od * 8 + (tc - 2) * 2 + 1)
                            s.activation(ev_a[:, (tc % 2) * 512:(tc % 2 + 1) * 512],
                                         pb[:, tc * 512:(tc + 1) * 512], AF.Silu)
                            s.activation(ev_b[:, (tc % 2) * 512:(tc % 2 + 1) * 512],
                                         pb[:, 2048 + tc * 512:2048 + (tc + 1) * 512],
                                         AF.Silu).then_inc(sS, 2)

                nS += OD * 8

                @blk.vector
                def _(v):
                    for od in range(OD):
                        for tc in range(TC):
                            v.wait_ge(sS, S2 + od * 8 + 2 * (tc + 1))
                            v.tensor_mul(kv_t[:, (tc % 2) * 512:(tc % 2 + 1) * 512],
                                         ev_a[:, (tc % 2) * 512:(tc % 2 + 1) * 512],
                                         ev_b[:, (tc % 2) * 512:(tc % 2 + 1) * 512]).then_inc(sV, 1)
                            v.wait_ge(sT, T2 + od * 8 + TC + tc + 1)
                            if tc >= 2:
                                v.wait_ge(dA, A2 + 16 * (od * 4 + tc - 1))
                            v.tensor_mul(u_t[:, (tc % 2) * 512:(tc % 2 + 1) * 512],
                                         kv_t[:, (tc % 2) * 512:(tc % 2 + 1) * 512],
                                         pb[:, tc * 512:(tc + 1) * 512]).then_inc(sV, 1)

                nV += OD * 8

                @blk.sync
                def _(sync):
                    for od in range(OD):
                        for tc in range(TC):
                            sync.wait_ge(sV, V2 + od * 8 + tc * 2 + 2)
                            sync.dma_start(
                                bass.AP(u_scr, od * 128 * T + tc * 512, [[T, 128], [1, 512]]),
                                u_t[:, (tc % 2) * 512:(tc % 2 + 1) * 512]).then_inc(dA, 16)

                ndA += 16 * OD * TC

            _sa2.close()
            a2_dA, a2_sT, a2_sV, a2_sS = ndA, nT, nV, nS

            _sb0 = contextlib.ExitStack()
            EB0 = _sb0.enter_context
            us_sb = EB0(nc.sbuf_tensor("us_sb", [128, HD * CT], BF16))
            ph_sb = EB0(nc.sbuf_tensor("ph_sb", [128, HD * CT], F32))
            co_sb = EB0(nc.sbuf_tensor("co_sb", [128, 4 * CT], F32, side="right"))
            ks_sb = EB0(nc.sbuf_tensor("ks_sb", [128, 4 * CT], F32, side="right"))
            qs_sb = EB0(nc.sbuf_tensor("qs_sb", [128, 2 * HD], F32, side="right"))
            tmp_sb = EB0(nc.sbuf_tensor("tmp_sb", [128, 128], F32, side="right"))

            if True:
                @blk.sync
                def _(sync):
                    sync.wait_ge(dA, a2_dA)
                    sync.wait_ge(sV, a2_sV)
                    sync.wait_ge(sS, a2_sS)
                    sync.wait_ge(sT, a2_sT)
                    for h in range(H):
                        sync.dma_start(
                            bass.AP(us_sb, h * 8 * (HD * CT), [[HD * CT, 8], [CT, HD], [1, CT]]),
                            bass.AP(u_scr, h * HD * T, [[CT, 8], [T, HD], [1, CT]])).then_inc(dB, 16)
                    for e in range(2):
                        sync.dma_start(
                            bass.AP(co_sb, e * CT, [[4 * CT, 128], [1, CT]]),
                            bass.AP(coef_scr, e * H * T, [[CT, 128], [1, CT]])).then_inc(dB, 16)

                ndB += 16 * 18
                b0_dB = ndB

                @blk.vector
                def _(v):
                    v.wait_ge(dB, b0_dB)
                    v.tensor_scalar(co_sb[:, 3 * CT:4 * CT], co_sb[:, 0:CT],
                                    (1 - MIX) / 2, None, ALU.mult)
                    v.tensor_scalar(co_sb[:, 2 * CT:3 * CT], co_sb[:, CT:2 * CT],
                                    (1 - MIX) / 2, None, ALU.mult)
                    v.tensor_scalar(co_sb[:, 0:CT], co_sb[:, 0:CT],
                                    (1 + MIX) / 2, None, ALU.mult)
                    v.tensor_scalar(co_sb[:, CT:2 * CT], co_sb[:, CT:2 * CT],
                                    (1 + MIX) / 2, None, ALU.mult)
                    v.memset(qs_sb[:, 0:HD], 0.0)
                    v.tensor_copy(bass.AP(ph_sb, 0, [[HD * CT, 128], [CT, HD]]),
                                  bass.AP(us_sb, 0, [[HD * CT, 128], [CT, HD]]))
                    for t_ in range(1, CT):
                        pr = bass.AP(ph_sb, t_ - 1, [[HD * CT, 128], [CT, HD]])
                        ut_ = bass.AP(us_sb, t_, [[HD * CT, 128], [CT, HD]])
                        qprev = qs_sb[:, ((t_ - 1) % 2) * HD:((t_ - 1) % 2 + 1) * HD]
                        qcur = qs_sb[:, (t_ % 2) * HD:(t_ % 2 + 1) * HD]
                        v.scalar_tensor_tensor(tmp_sb[:, 0:HD], qprev,
                                               co_sb[:, CT + t_:CT + t_ + 1], ut_,
                                               ALU.mult, ALU.add)
                        v.scalar_tensor_tensor(
                            bass.AP(ph_sb, t_, [[HD * CT, 128], [CT, HD]]), pr,
                            co_sb[:, t_:t_ + 1], tmp_sb[:, 0:HD], ALU.mult, ALU.add)
                        v.tensor_scalar(tmp_sb[:, HD:2 * HD], pr,
                                        co_sb[:, 2 * CT + t_:2 * CT + t_ + 1], None, ALU.mult)
                        v.scalar_tensor_tensor(qcur, qprev,
                                               co_sb[:, 3 * CT + t_:3 * CT + t_ + 1],
                                               tmp_sb[:, HD:2 * HD], ALU.mult, ALU.add)
                    v.tensor_copy(qs_sb[:, 0:HD],
                                  bass.AP(ph_sb, CT - 1, [[HD * CT, 128], [CT, HD]]))
                    src_, dst = co_sb, ks_sb
                    for rnd in range(8):
                        s_ = 1 << rnd
                        n = CT - s_
                        for e in range(4):
                            v.tensor_copy(dst[:, e * CT:e * CT + s_], src_[:, e * CT:e * CT + s_])
                        a00 = src_[:, 0 * CT + s_:1 * CT]
                        a01 = src_[:, 1 * CT + s_:2 * CT]
                        a10 = src_[:, 2 * CT + s_:3 * CT]
                        a11 = src_[:, 3 * CT + s_:4 * CT]
                        b00 = src_[:, 0 * CT:0 * CT + n]
                        b01 = src_[:, 1 * CT:1 * CT + n]
                        b10 = src_[:, 2 * CT:2 * CT + n]
                        b11 = src_[:, 3 * CT:3 * CT + n]
                        d00 = dst[:, 0 * CT + s_:1 * CT]
                        d01 = dst[:, 1 * CT + s_:2 * CT]
                        d10 = dst[:, 2 * CT + s_:3 * CT]
                        d11 = dst[:, 3 * CT + s_:4 * CT]
                        v.tensor_mul(d00, a00, b00)
                        v.tensor_mul(d01, a01, b10)
                        v.tensor_add(d00, d00, d01)
                        v.tensor_mul(d01, a00, b01)
                        v.tensor_mul(d10, a01, b11)
                        v.tensor_add(d01, d01, d10)
                        v.tensor_mul(d10, a10, b00)
                        v.tensor_mul(d11, a11, b10)
                        v.tensor_add(d10, d10, d11)
                        v.tensor_mul(d11, a10, b01)
                        for off in range(0, n, 128):
                            w = min(128, n - off)
                            v.tensor_mul(tmp_sb[:, 0:w], a11[:, off:off + w], b11[:, off:off + w])
                            v.tensor_add(dst[:, 3 * CT + s_ + off:3 * CT + s_ + off + w],
                                         dst[:, 3 * CT + s_ + off:3 * CT + s_ + off + w],
                                         tmp_sb[:, 0:w])
                        src_, dst = dst, src_
                    v.tensor_copy(tmp_sb[:, 0:1], co_sb[:, 0:1]).then_inc(sV, 1)

                nV += 1
                ks_V = nV

            T3, S3 = nT, nS
            G3 = nG
            if True:
                @blk.gpsimd
                def _(g):
                    g.wait_ge(sT, T3)
                    for kt in range(KT):
                        g.dma_start(wp_sb[:, kt * D:(kt + 1) * D],
                                    wqT[kt * 128:(kt + 1) * 128, :]).then_inc(dW, 16)
                        g.dma_start(wp_sb[:, (KT + kt) * D:(KT + kt + 1) * D],
                                    wgT[kt * 128:(kt + 1) * 128, :]).then_inc(dW, 16)
                    i = 0
                    for od in range(OD):
                        for tc in range(TC):
                            half, j = tc // 2, tc % 2
                            g.wait_ge(sS, S3 + 1 + od * 8 + half * 4 + 2 + j + 1)
                            if i >= 2:
                                g.wait_ge(dG, 16 * (i - 1))
                            g.tensor_mul(qgt_sb[:, (i % 2) * 512:(i % 2 + 1) * 512],
                                         ev_a[:, j * 512:(j + 1) * 512],
                                         ev_b[:, j * 512:(j + 1) * 512]).then_inc(sG, 1)
                            g.dma_start(
                                bass.AP(qg_scr, od * 128 * T + tc * 512, [[T, 128], [1, 512]]),
                                qgt_sb[:, (i % 2) * 512:(i % 2 + 1) * 512]).then_inc(dG, 16)
                            i += 1

                qg_dW = ndW + 16 * 2 * KT
                ndW = qg_dW
                nG += OD * TC
                ndG_a3 = 16 * OD * TC

                @blk.tensor
                def _(t):
                    t.wait_ge(dW, qg_dW)
                    t.wait_ge(sV, a2_sV)
                    for kt in range(KT):
                        for nh in range(2):
                            t.matmul(pb[0:1, nh * 512:(nh + 1) * 512], ones_p1[:],
                                     wp_sb[:, kt * D + nh * 512:kt * D + (nh + 1) * 512],
                                     start=(kt == 0), stop=(kt == KT - 1))
                            mm = t.matmul(pb[0:1, 1024 + nh * 512:1536 + nh * 512], ones_p1[:],
                                     wp_sb[:, (KT + kt) * D + nh * 512:(KT + kt) * D + (nh + 1) * 512],
                                     start=(kt == 0), stop=(kt == KT - 1))
                    mm.then_inc(sT, 1)
                    t.wait_ge(sS, S3 + 1)
                    for od in range(OD):
                        if od > 0:
                            t.wait_ge(sS, S3 + 1 + od * 8)
                        for kt in range(KT):
                            for tc in range(TC):
                                t.matmul(pb[:, tc * 512:(tc + 1) * 512],
                                         wp_sb[:, kt * D + od * 128:kt * D + od * 128 + 128],
                                         xl_sb[:, kt * T + tc * 512:kt * T + (tc + 1) * 512],
                                         start=(kt == 0), stop=False)
                            for tc in range(TC):
                                t.matmul(pb[:, 2048 + tc * 512:2048 + (tc + 1) * 512],
                                         wp_sb[:, (KT + kt) * D + od * 128:(KT + kt) * D + od * 128 + 128],
                                         xl_sb[:, kt * T + tc * 512:kt * T + (tc + 1) * 512],
                                         start=(kt == 0), stop=False)
                        for tc in range(TC):
                            t.matmul(pb[:, tc * 512:(tc + 1) * 512],
                                     cs_q[0:1, od * 128:(od + 1) * 128],
                                     negs16[0:1, tc * 512:(tc + 1) * 512],
                                     start=False, stop=True)
                            t.matmul(pb[:, 2048 + tc * 512:2048 + (tc + 1) * 512],
                                     cs_g[0:1, od * 128:(od + 1) * 128],
                                     negs16[0:1, tc * 512:(tc + 1) * 512],
                                     start=False, stop=True).then_inc(sT, 1)

                nT += 1 + OD * TC

                @blk.scalar
                def _(s):
                    s.wait_ge(sT, T3 + 1)
                    s.activation(cs_q[:], pb[0:1, 0:1024], AF.Copy)
                    s.activation(cs_g[:], pb[0:1, 1024:2048], AF.Copy).then_inc(sS, 1)
                    for od in range(OD):
                        for half in range(2):
                            for j in range(2):
                                tc = half * 2 + j
                                s.wait_ge(sT, T3 + 1 + od * 4 + tc + 1)
                                prev = od * 4 + tc - 2
                                if prev >= 0:
                                    s.wait_ge(sG, G3 + prev + 1)
                                s.activation(ev_a[:, j * 512:(j + 1) * 512],
                                             pb[:, tc * 512:(tc + 1) * 512],
                                             AF.Silu).then_inc(sS, 1)
                            for j in range(2):
                                tc = half * 2 + j
                                s.activation(ev_b[:, j * 512:(j + 1) * 512],
                                             pb[:, 2048 + tc * 512:2048 + (tc + 1) * 512],
                                             AF.Sigmoid).then_inc(sS, 1)

                nS += 1 + OD * 8

            a3_sT, a3_sS, a3_sG = nT, nS, nG


            _sb1 = contextlib.ExitStack()
            EB1 = _sb1.enter_context
            wch_sb = EB1(nc.sbuf_tensor("wch_sb", [16, 8 * 128], F32))
            rch_sb = EB1(nc.sbuf_tensor("rch_sb", [16, 32], F32))
            ech_sb = EB1(nc.sbuf_tensor("ech_sb", [16, 9 * 128], F32))
            rp_sb = EB1(nc.sbuf_tensor("rp_sb", [16, 8], F32))
            agw_sb = EB1(nc.sbuf_tensor("agw_sb", [16, 8 * 128], F32))
            agr_sb = EB1(nc.sbuf_tensor("agr_sb", [16, 32], F32))
            ent_sb = EB1(nc.sbuf_tensor("ent_sb", [16, 128], F32))
            ecc_sb = ech_sb
            eg_sb = agw_sb

            if True:
                @blk.sync
                def _(sync):
                    sync.wait_ge(sV, ks_V)
                    sync.wait_ge(sT, a3_sT)
                    sync.wait_ge(sS, a3_sS)
                    sync.wait_ge(sG, a3_sG)
                    sync.dma_start(bass.AP(wch_scr, 0, [[128, 128], [1, 128]]),
                                   qs_sb[:]).then_inc(dB, 16)
                    sync.dma_start(bass.AP(wch_scr, 0, [[128, 128], [1, 64]]),
                                   qs_sb[:, 0:64]).then_inc(dB, 16)
                    for e in range(4):
                        sync.dma_start(bass.AP(rch_scr, e * 128, [[1, 128]]),
                                       co_sb[:, e * CT + CT - 1:(e * CT) + CT]).then_inc(dB, 16)
                    sync.wait_ge(dB, b0_dB + 16 * 6)
                    for h in range(H):
                        sync.dma_start(
                            bass.AP(rrow_scr, h * 2 * T, [[CT, 8], [T, 2], [1, CT]]),
                            bass.AP(co_sb, (h * 8) * (4 * CT),
                                    [[4 * CT, 8], [CT, 2], [1, CT]])).then_inc(dB, 16)
                    sync.dma_start(wch_sb[:], bass.AP(wch_scr, 0, [[1024, 16], [1, 1024]])).then_inc(dB, 16)
                    sync.dma_start(rch_sb[:], bass.AP(rch_scr, 0, [[8, 16], [128, 4], [1, 8]])).then_inc(dB, 16)

                ndB += 16 * (6 + 16 + 2)
                b1_dB = ndB

                @blk.vector
                def _(v):
                    v.scalar_tensor_tensor(ph_sb[:], us_sb[:], -1.0, ph_sb[:],
                                           ALU.mult, ALU.add)
                    v.nop()
                    v.memset(tmp_sb[:, 0:1], 0.0).then_inc(sV, 1)
                    v.nop()
                    v.wait_ge(dB, b1_dB)
                    v.memset(ech_sb[:, 0:128], 0.0)
                    for c in range(1, 9):
                        pP = ech_sb[:, (c - 1) * 128:(c - 1) * 128 + 64]
                        pQ = ech_sb[:, (c - 1) * 128 + 64:c * 128]
                        wP = wch_sb[:, (c - 1) * 128:(c - 1) * 128 + 64]
                        wQ = wch_sb[:, (c - 1) * 128 + 64:c * 128]
                        r0 = rch_sb[:, 0 * 8 + (c - 1):0 * 8 + c]
                        r1 = rch_sb[:, 1 * 8 + (c - 1):1 * 8 + c]
                        r2_ = rch_sb[:, 2 * 8 + (c - 1):2 * 8 + c]
                        r3 = rch_sb[:, 3 * 8 + (c - 1):3 * 8 + c]
                        v.scalar_tensor_tensor(tmp_sb[0:16, 0:64], pQ, r1, wP, ALU.mult, ALU.add)
                        v.scalar_tensor_tensor(ech_sb[:, c * 128:c * 128 + 64], pP, r0,
                                               tmp_sb[0:16, 0:64], ALU.mult, ALU.add)
                        v.scalar_tensor_tensor(tmp_sb[0:16, 64:128], pQ, r3, wQ, ALU.mult, ALU.add)
                        v.scalar_tensor_tensor(ech_sb[:, c * 128 + 64:(c + 1) * 128], pP, r2_,
                                               tmp_sb[0:16, 64:128], ALU.mult, ALU.add)
                    v.memset(rp_sb[:], 0.0)
                    v.tensor_scalar(rp_sb[:, 0:1], ones_p1f[0:16, 0:1], 1.0, None, ALU.mult)
                    v.tensor_scalar(rp_sb[:, 3:4], ones_p1f[0:16, 0:1], 1.0, None, ALU.mult)
                    for c in range(8):
                        pi = (c % 2) * 4
                        po = ((c + 1) % 2) * 4
                        r0 = rch_sb[:, 0 * 8 + c:0 * 8 + c + 1]
                        r1 = rch_sb[:, 1 * 8 + c:1 * 8 + c + 1]
                        r2_ = rch_sb[:, 2 * 8 + c:2 * 8 + c + 1]
                        r3 = rch_sb[:, 3 * 8 + c:3 * 8 + c + 1]
                        v.tensor_scalar(tmp_sb[0:16, 0:1], rp_sb[:, pi + 2:pi + 3], r1, None, ALU.mult)
                        v.scalar_tensor_tensor(rp_sb[:, po + 0:po + 1], rp_sb[:, pi + 0:pi + 1],
                                               r0, tmp_sb[0:16, 0:1], ALU.mult, ALU.add)
                        v.tensor_scalar(tmp_sb[0:16, 0:1], rp_sb[:, pi + 3:pi + 4], r1, None, ALU.mult)
                        v.scalar_tensor_tensor(rp_sb[:, po + 1:po + 2], rp_sb[:, pi + 1:pi + 2],
                                               r0, tmp_sb[0:16, 0:1], ALU.mult, ALU.add)
                        v.tensor_scalar(tmp_sb[0:16, 0:1], rp_sb[:, pi + 2:pi + 3], r3, None, ALU.mult)
                        v.scalar_tensor_tensor(rp_sb[:, po + 2:po + 3], rp_sb[:, pi + 0:pi + 1],
                                               r2_, tmp_sb[0:16, 0:1], ALU.mult, ALU.add)
                        v.tensor_scalar(tmp_sb[0:16, 0:1], rp_sb[:, pi + 3:pi + 4], r3, None, ALU.mult)
                        v.scalar_tensor_tensor(rp_sb[:, po + 3:po + 4], rp_sb[:, pi + 1:pi + 2],
                                               r2_, tmp_sb[0:16, 0:1], ALU.mult, ALU.add)
                    v.tensor_copy(tmp_sb[0:16, 0:1], rp_sb[:, 0:1]).then_inc(sV, 1)

                nV += 1
                zloc_V = nV
                nV += 1
                chain_V = nV

                @blk.gpsimd
                def _(g):
                    g.wait_ge(sV, chain_V)
                    g.dma_start(bass.AP(agin_scr, 0, [[128, 16], [1, 128]]),
                                ech_sb[:, 8 * 128:9 * 128]).then_inc(dG, 16)
                    g.dma_start(bass.AP(agin_scr, 2048, [[4, 16], [1, 4]]),
                                rp_sb[:, 0:4]).then_inc(dG, 16)
                    g.wait_ge(dG, ndG_a3 + 32)
                    g.collective_compute(
                        "AllGather", ALU.bypass, [[0, 1, 2, 3, 4, 5, 6, 7]],
                        [agin_scr.ap().opt()], [agout_scr.ap().opt()]).then_inc(cc, 1)
                    g.wait_ge(cc, 1)
                    g.dma_start(agw_sb[:],
                                bass.AP(agout_scr, 0, [[128, 16], [2112, 8], [1, 128]])).then_inc(dG, 16)
                    g.dma_start(agr_sb[:],
                                bass.AP(agout_scr, 2048, [[4, 16], [2112, 8], [1, 4]])).then_inc(dG, 16)

                ncc += 1

                @blk.vector
                def _(v):
                    v.wait_ge(dG, ndG_a3 + 64)
                    v.memset(ecc_sb[:, 0:128], 0.0)
                    for k in range(1, 8):
                        if k == 4:
                            v.memset(ecc_sb[:, 4 * 128:5 * 128], 0.0)
                            continue
                        pP = ecc_sb[:, (k - 1) * 128:(k - 1) * 128 + 64]
                        pQ = ecc_sb[:, (k - 1) * 128 + 64:k * 128]
                        wP = agw_sb[:, (k - 1) * 128:(k - 1) * 128 + 64]
                        wQ = agw_sb[:, (k - 1) * 128 + 64:k * 128]
                        r0 = agr_sb[:, (k - 1) * 4 + 0:(k - 1) * 4 + 1]
                        r1 = agr_sb[:, (k - 1) * 4 + 1:(k - 1) * 4 + 2]
                        r2_ = agr_sb[:, (k - 1) * 4 + 2:(k - 1) * 4 + 3]
                        r3 = agr_sb[:, (k - 1) * 4 + 3:(k - 1) * 4 + 4]
                        v.scalar_tensor_tensor(tmp_sb[0:16, 0:64], pQ, r1, wP, ALU.mult, ALU.add)
                        v.scalar_tensor_tensor(ecc_sb[:, k * 128:k * 128 + 64], pP, r0,
                                               tmp_sb[0:16, 0:64], ALU.mult, ALU.add)
                        v.scalar_tensor_tensor(tmp_sb[0:16, 64:128], pQ, r3, wQ, ALU.mult, ALU.add)
                        v.scalar_tensor_tensor(ecc_sb[:, k * 128 + 64:(k + 1) * 128], pP, r2_,
                                               tmp_sb[0:16, 64:128], ALU.mult, ALU.add)
                    v.memset(ent_sb[:], 0.0)
                    for k in range(8):
                        v.scalar_tensor_tensor(ent_sb[:], ecc_sb[:, k * 128:(k + 1) * 128],
                                               sel_sb[:, k:k + 1], ent_sb[:], ALU.mult, ALU.add)
                    v.tensor_copy(eg_sb[:, 0:128], ent_sb[:])
                    for c in range(1, 8):
                        pP = eg_sb[:, (c - 1) * 128:(c - 1) * 128 + 64]
                        pQ = eg_sb[:, (c - 1) * 128 + 64:c * 128]
                        wP = wch_sb[:, (c - 1) * 128:(c - 1) * 128 + 64]
                        wQ = wch_sb[:, (c - 1) * 128 + 64:c * 128]
                        r0 = rch_sb[:, 0 * 8 + (c - 1):0 * 8 + c]
                        r1 = rch_sb[:, 1 * 8 + (c - 1):1 * 8 + c]
                        r2_ = rch_sb[:, 2 * 8 + (c - 1):2 * 8 + c]
                        r3 = rch_sb[:, 3 * 8 + (c - 1):3 * 8 + c]
                        v.scalar_tensor_tensor(tmp_sb[0:16, 0:64], pQ, r1, wP, ALU.mult, ALU.add)
                        v.scalar_tensor_tensor(eg_sb[:, c * 128:c * 128 + 64], pP, r0,
                                               tmp_sb[0:16, 0:64], ALU.mult, ALU.add)
                        v.scalar_tensor_tensor(tmp_sb[0:16, 64:128], pQ, r3, wQ, ALU.mult, ALU.add)
                        v.scalar_tensor_tensor(eg_sb[:, c * 128 + 64:(c + 1) * 128], pP, r2_,
                                               tmp_sb[0:16, 64:128], ALU.mult, ALU.add)
                    v.nop()
                    v.memset(tmp_sb[:, 1:2], 0.0).then_inc(sV, 1)
                    v.nop()

                nV += 1
                eg_V = nV

                @blk.sync
                def _(sync):
                    sync.wait_ge(sV, zloc_V)
                    for h in range(H // 2):
                        sync.dma_start(
                            bass.AP(zl_scr, h * HD * T, [[CT, 8], [T, HD], [1, CT]]),
                            bass.AP(ph_sb, h * 8 * (HD * CT), [[HD * CT, 8], [CT, HD], [1, CT]])).then_inc(dB, 16)
                    for c in range(NCH):
                        sync.dma_start(
                            bass.AP(corrl_scr, c * 16 * D, [[1, 16384]]),
                            zro_sb[:]).then_inc(dB, 16)
                    sync.wait_ge(sV, eg_V)
                    sync.wait_ge(dB, b1_dB + 16 * (H + 2 * NCH))
                    for comp in range(2):
                        sync.dma_start(
                            bass.AP(corrl_scr, comp * 1024, [[2112, 16], [32768, 8], [1, 64]]),
                            bass.AP(eg_sb, comp * 64, [[1024, 16], [128, 8], [1, 64]])).then_inc(dB, 16)

                @blk.gpsimd
                def _(g):
                    g.wait_ge(sV, zloc_V)
                    for h in range(H // 2, H):
                        g.dma_start(
                            bass.AP(zl_scr, h * HD * T, [[CT, 8], [T, HD], [1, CT]]),
                            bass.AP(ph_sb, h * 8 * (HD * CT), [[HD * CT, 8], [CT, HD], [1, CT]])).then_inc(dB, 16)
                    for c in range(NCH, 2 * NCH):
                        g.dma_start(
                            bass.AP(corrl_scr, c * 16 * D, [[1, 16384]]),
                            zro_sb[:]).then_inc(dB, 16)

                ndB += 16 * (H + 2 * NCH + 2)

            _sb1.close()
            _sb0.close()

        zb_dB = ndB
        b_sV = nV

        with contextlib.ExitStack() as _sc_:
            EC = _sc_.enter_context
            x1_sb = EC(nc.sbuf_tensor("x1_sb", [128, OD * T], F32))
            mrow16 = EC(nc.sbuf_tensor("mrow16", [1, T], BF16))
            _sc2_ = contextlib.ExitStack()
            corr_sb = _sc2_.enter_context(nc.sbuf_tensor("corr_sb", [128, OD * T], BF16))
            wo_sb = _sc2_.enter_context(nc.sbuf_tensor("wo_sb", [128, KT * D], BF16))
            with contextlib.ExitStack() as _sd_:
                ED = _sd_.enter_context
                cl_sb = ED(nc.sbuf_tensor("cl_sb", [32, NCH * D], BF16))
                rr_sb = ED(nc.sbuf_tensor("rr_sb", [32, T], BF16))
                TB2, SB2 = nT, nS
                if True:
                    @blk.gpsimd
                    def _(g):
                        g.wait_ge(dB, zb_dB)
                        g.wait_ge(sV, b_sV)
                        g.dma_start(cl_sb[:], bass.AP(corrl_scr, 0,
                                    [[1024, 32], [32768, 8], [1, 1024]])).then_inc(dG, 16)
                        g.dma_start(rr_sb[:], bass.AP(rrow_scr, 0,
                                    [[T, 32], [1, T]])).then_inc(dG, 16)
                        for kt in range(KT):
                            g.dma_start(wo_sb[:, kt * D:(kt + 1) * D],
                                        woT[kt * 128:(kt + 1) * 128, :]).then_inc(dW, 16)

                    ndW += 16 * KT
                    wo_dW = ndW

                    ndB += 0
                    c0_dB = ndB

                    @blk.vector
                    def _(v):
                        v.wait_ge(dW, wo_dW)
                        for kt in range(KT):
                            v.tensor_reduce(rsumf[:, kt:kt + 1],
                                            wo_sb[:, kt * D:(kt + 1) * D], AX.X, ALU.add)
                        v.tensor_copy(bass.AP(rsum16, 0, [[2 * KT, 128], [2, KT]]),
                                      rsumf[:])
                        v.nop()
                        v.tensor_copy(bass.AP(rsum16, 0, [[2 * KT, 128], [2, KT]]),
                                      rsumf[:]).then_inc(sV, 1)
                        v.nop()

                    nV += 1
                    rs_V = nV

                    @blk.tensor
                    def _(t):
                        t.wait_ge(dG, ndG_a3 + 96)
                        i = 0
                        for od in range(OD):
                            for ch in range(NCH):
                                if i >= 2:
                                    t.wait_ge(sS, SB2 + i - 1)
                                t.matmul(pb[:, 3072 + (i % 2) * 512:3072 + (i % 2) * 512 + 256],
                                         cl_sb[:, ch * D + od * 128:ch * D + (od + 1) * 128],
                                         rr_sb[:, ch * CT:(ch + 1) * CT],
                                         start=True, stop=True).then_inc(sT, 1)
                                i += 1

                    nT += OD * NCH

                    @blk.scalar
                    def _(s):
                        i = 0
                        for od in range(OD):
                            for ch in range(NCH):
                                s.wait_ge(sT, TB2 + i + 1)
                                s.activation(corr_sb[:, od * T + ch * CT:od * T + (ch + 1) * CT],
                                             pb[:, 3072 + (i % 2) * 512:3072 + (i % 2) * 512 + 256],
                                             AF.Copy).then_inc(sS, 1)
                                i += 1

                    nS += OD * NCH

            with contextlib.ExitStack() as _se_:
                EE = _se_.enter_context
                zl_t = EE(nc.sbuf_tensor("zl_t", [128, 4 * 512], F32))
                qgc_t = EE(nc.sbuf_tensor("qgc_t", [128, 4 * 512], BF16))
                xr_t = EE(nc.sbuf_tensor("xr_t", [128, 2 * OD * 512], F32))
                og_t = EE(nc.sbuf_tensor("og_t", [128, OD * 512], BF16))
                at_t = EE(nc.sbuf_tensor("at_t", [128, OD * 512], BF16))
                sq_t = EE(nc.sbuf_tensor("sq_t", [128, 2 * 512], BF16))
                r2_sb = EE(nc.sbuf_tensor("r2_sb", [128, 512], F32))
                srow_a = EE(nc.sbuf_tensor("srow_a2", [1, 512], F32))[0:1, :]
                srow_c = EE(nc.sbuf_tensor("srow_c2", [1, 512], F32))[0:1, :]
                TC0, VC0, SC0 = nT, nV, nS
                VPT, TPT, SPT = 18, 18, 17

                if True:
                    @blk.sync
                    def _(sync):
                        sync.wait_ge(sT, TB2 + OD * NCH)
                        sync.wait_ge(dG, ndG_a3 + 96)
                        csem = [dCa, dCb, dCd, dCe]
                        for tc in range(TC):
                            for od in range(OD):
                                if tc * OD + od >= 4:
                                    prev = tc * OD + od - 4
                                    ptc, pod = prev // OD, prev % OD
                                    sync.wait_ge(sV, VC0 + ptc * VPT + (pod + 1) * 1)
                                sync.dma_start(
                                    zl_t[:, (od % 4) * 512:(od % 4 + 1) * 512],
                                    bass.AP(zl_scr, od * 128 * T + tc * 512,
                                            [[T, 128], [1, 512]])).then_inc(csem[od % 4], 16)
                                sync.dma_start(
                                    qgc_t[:, (od % 4) * 512:(od % 4 + 1) * 512],
                                    bass.AP(qg_scr, od * 128 * T + tc * 512,
                                            [[T, 128], [1, 512]])).then_inc(csem[od % 4], 16)
                            if tc >= 2:
                                sync.wait_ge(sV, VC0 + (tc - 1) * VPT)
                            for od in range(OD):
                                sync.dma_start(
                                    xr_t[:, (tc % 2) * OD * 512 + od * 512:(tc % 2) * OD * 512 + (od + 1) * 512],
                                    xT[od * 128:(od + 1) * 128, tc * 512:(tc + 1) * 512]).then_inc(dCc, 16)

                    @blk.vector
                    def _(v):
                        v.wait_ge(sS, SB2 + OD * NCH)
                        for tc in range(TC):
                            base_t = TC0 + tc * TPT
                            if tc >= 1:
                                v.wait_ge(sT, TC0 + (tc - 1) * TPT + 9)
                            for od in range(OD):
                                nz = tc * 2 + od // 4 + 1
                                v.wait_ge([dCa, dCb, dCd, dCe][od % 4], 32 * nz)
                                v.tensor_add(zl_t[:, (od % 4) * 512:(od % 4 + 1) * 512],
                                             zl_t[:, (od % 4) * 512:(od % 4 + 1) * 512],
                                             corr_sb[:, od * T + tc * 512:od * T + (tc + 1) * 512])
                                v.tensor_mul(og_t[:, od * 512:(od + 1) * 512],
                                             zl_t[:, (od % 4) * 512:(od % 4 + 1) * 512],
                                             qgc_t[:, (od % 4) * 512:(od % 4 + 1) * 512]).then_inc(sV, 1)
                            v.wait_ge(sT, base_t + 1)
                            v.tensor_scalar(mrow16[0:1, tc * 512:(tc + 1) * 512],
                                            pb[0:1, 2560:3072], -1.0 / D, None, ALU.mult).then_inc(sV, 1)
                            v.wait_ge(sT, base_t + TPT)
                            v.tensor_copy(r2_sb[:], pb[:, 3072:3584]).then_inc(sV, 1)
                            v.wait_ge(dCc, 16 * OD * (tc + 1))
                            for od in range(OD):
                                v.tensor_mul(x1_sb[:, od * T + tc * 512:od * T + (tc + 1) * 512],
                                             at_t[:, od * 512:(od + 1) * 512], r2_sb[:])
                                v.tensor_add(x1_sb[:, od * T + tc * 512:od * T + (tc + 1) * 512],
                                             x1_sb[:, od * T + tc * 512:od * T + (tc + 1) * 512],
                                             xr_t[:, (tc % 2) * OD * 512 + od * 512:(tc % 2) * OD * 512 + (od + 1) * 512]).then_inc(sV, 1)

                    nV += TC * VPT

                    @blk.tensor
                    def _(t):
                        t.wait_ge(sV, rs_V)
                        for tc in range(TC):
                            base_v = VC0 + tc * VPT
                            base_s = SC0 + tc * SPT
                            t.wait_ge(sV, base_v + 8)
                            for kt in range(KT):
                                mm = t.matmul(pb[0:1, 2560:3072], rsum16[:, 2 * kt:2 * kt + 1],
                                              og_t[:, kt * 512:(kt + 1) * 512],
                                              start=(kt == 0), stop=(kt == KT - 1))
                            mm.then_inc(sT, 1)
                            for op_ in range(4):
                                if op_ >= 2:
                                    t.wait_ge(sS, base_s + (op_ - 2) * 2 + 2)
                                elif tc >= 1:
                                    t.wait_ge(sS, SC0 + (tc - 1) * SPT + (op_ + 2) * 2 + 2)
                                for kt in range(KT):
                                    for j in range(2):
                                        od = 2 * op_ + j
                                        bslot = (op_ % 2) * 2 + j
                                        t.matmul(pb[:, bslot * 512:(bslot + 1) * 512],
                                                 wo_sb[:, kt * D + od * 128:kt * D + od * 128 + 128],
                                                 og_t[:, kt * 512:(kt + 1) * 512],
                                                 start=(kt == 0), stop=False)
                                t.wait_ge(sV, base_v + 9)
                                for j in range(2):
                                    bslot = (op_ % 2) * 2 + j
                                    t.matmul(pb[:, bslot * 512:(bslot + 1) * 512],
                                             ones_1r16[:], mrow16[0:1, tc * 512:(tc + 1) * 512],
                                             start=False, stop=True).then_inc(sT, 1)
                            for od in range(OD):
                                t.wait_ge(sS, base_s + 8 + od + 1)
                                t.matmul(pb[0:1, 2048:2560], ones_p1[:],
                                         sq_t[:, (od % 2) * 512:(od % 2 + 1) * 512],
                                         start=(od == 0), stop=(od == OD - 1)).then_inc(sT, 1)
                            t.wait_ge(sS, base_s + SPT)
                            t.matmul(pb[:, 3072:3584], ones_1r[:],
                                     srow_c[:, 0:512],
                                     start=True, stop=True).then_inc(sT, 1)

                    nT += TC * TPT

                    @blk.scalar
                    def _(s):
                        for tc in range(TC):
                            base_t = TC0 + tc * TPT
                            for op_ in range(4):
                                for j in range(2):
                                    od = 2 * op_ + j
                                    bslot = (op_ % 2) * 2 + j
                                    s.wait_ge(sT, base_t + 1 + op_ * 2 + j + 1)
                                    if tc >= 1:
                                        s.wait_ge(sV, VC0 + (tc - 1) * VPT + 10 + od + 1)
                                    s.activation(at_t[:, od * 512:(od + 1) * 512],
                                                 pb[:, bslot * 512:(bslot + 1) * 512],
                                                 AF.Copy).then_inc(sS, 1)
                            for od in range(OD):
                                if od >= 2:
                                    s.wait_ge(sT, base_t + 1 + 8 + od - 1)
                                s.activation(sq_t[:, (od % 2) * 512:(od % 2 + 1) * 512],
                                             at_t[:, od * 512:(od + 1) * 512],
                                             AF.Square).then_inc(sS, 1)
                            s.wait_ge(sT, base_t + 1 + 8 + 8)
                            s.activation(srow_a[:, 0:512], pb[0:1, 2048:2560],
                                         AF.Copy, scale=cst_sb[0:1, 2:3])
                            s.activation(srow_c[:, 0:512],
                                         srow_a[:, 0:512], AF.Ln, bias=cst_sb[0:1, 0:1])
                            s.activation(srow_c[:, 0:512],
                                         srow_c[:, 0:512], AF.Exp,
                                         scale=cst_sb[0:1, 1:2]).then_inc(sS, 1)

                    nS += TC * SPT

            c_sV, c_sS, c_sT = nV, nS, nT
            _sc2_.close()

            with contextlib.ExitStack() as _sf_:
                EF = _sf_.enter_context
                x1l_sb = EF(nc.sbuf_tensor("x1l_sb", [128, KT * T], BF16))
                rb_sb = EF(nc.sbuf_tensor("rb2_sb", [128, T], F32))
                negs16 = EF(nc.sbuf_tensor("negs16d", [1, T], BF16))
                cs_f = EF(nc.sbuf_tensor("cs_f", [1, D], BF16))
                wf_sb = EF(nc.sbuf_tensor("wf_sb", [128, KT * D], BF16))
                sf_sb = wf_sb
                y_t = EF(nc.sbuf_tensor("y_t", [128, OD * 512], BF16))
                sq2_t = EF(nc.sbuf_tensor("sq2_t", [128, 2 * 512], BF16))
                xsq2 = EF(nc.sbuf_tensor("xsq2", [128, T], BF16))
                ot_t = EF(nc.sbuf_tensor("ot_t", [128, 4 * 512], F32))
                r4_sb = rb_sb[:, 0:512]
                srow_a = EF(nc.sbuf_tensor("srow_a3", [1, T], F32))[0:1, :]
                srow_b = EF(nc.sbuf_tensor("srow_b3", [1, T], F32))[0:1, :]
                srow_c = EF(nc.sbuf_tensor("srow_c3", [1, T], F32))[0:1, :]
                stg_sb = EF(nc.sbuf_tensor("wfstg", [128, 2 * D], F32))
                TD0, VD0, SD0 = nT, nV, nS
                if True:
                    @blk.sync
                    def _(sync):
                        sync.wait_ge(sT, TD0)
                        sync.wait_ge(sV, c_sV)
                        sync.wait_ge(sS, c_sS)
                        for kt in range(KT):
                            if kt >= 2:
                                sync.wait_ge(sS, SD0 + kt - 1)
                            sync.dma_start(stg_sb[:, (kt % 2) * D:(kt % 2 + 1) * D],
                                           wfT[kt * 128:(kt + 1) * 128, :]
                                           ).then_inc(wA if kt % 2 == 0 else wB, 16)

                    @blk.scalar
                    def _(s):
                        for kt in range(KT):
                            s.wait_ge(wA if kt % 2 == 0 else wB,
                                      (nwA if kt % 2 == 0 else nwB) + 16 * (kt // 2 + 1))
                            s.sign(sf_sb[:, kt * D:(kt + 1) * D],
                                   stg_sb[:, (kt % 2) * D:(kt % 2 + 1) * D]).then_inc(sS, 1)
                        s.wait_ge(sV, c_sV)
                        for kt in range(KT):
                            if kt >= 1:
                                s.wait_ge(sT, TD0 + kt)
                            s.activation(xsq2[:, 0:T],
                                         x1_sb[:, kt * T:(kt + 1) * T],
                                         AF.Square).then_inc(sS, 1)

                    nS += 2 * KT
                    sgn_S = SD0 + KT

                    @blk.vector
                    def _(v):
                        v.wait_ge(sS, sgn_S)
                        for kt in range(KT):
                            v.tensor_reduce(rsumf[:, kt:kt + 1],
                                            sf_sb[:, kt * D:(kt + 1) * D], AX.X, ALU.add)
                        v.tensor_copy(bass.AP(rsum16, 0, [[2 * KT, 128], [2, KT]]),
                                      rsumf[:])
                        v.nop()
                        v.tensor_copy(bass.AP(rsum16, 0, [[2 * KT, 128], [2, KT]]),
                                      rsumf[:]).then_inc(sV, 1)
                        v.nop()

                    nV += 1
                    rs2_V = nV

                    @blk.tensor
                    def _(t):
                        t.wait_ge(sV, c_sV)
                        for kt in range(KT):
                            t.wait_ge(sS, sgn_S + kt + 1)
                            for tc in range(TC):
                                t.matmul(pb[0:1, tc * 512:(tc + 1) * 512], ones_p1f[:],
                                         x1_sb[:, kt * T + tc * 512:kt * T + (tc + 1) * 512],
                                         start=(kt == 0), stop=(kt == KT - 1))
                            for tc in range(TC):
                                mm = t.matmul(pb[0:1, 2048 + tc * 512:2048 + (tc + 1) * 512],
                                              ones_p1[:],
                                              xsq2[:, tc * 512:(tc + 1) * 512],
                                              start=(kt == 0), stop=(kt == KT - 1))
                            mm.then_inc(sT, 1)

                    nT += KT
                    d0_T = nT

                    @blk.vector
                    def _(v):
                        v.wait_ge(sT, d0_T)
                        v.tensor_scalar(srow_a, pb[0:1, 0:2048], 1.0 / D, None, ALU.mult)
                        v.tensor_scalar(srow_b, pb[0:1, 2048:4096], 1.0 / D, None, ALU.mult)
                        v.tensor_mul(srow_c, srow_a, srow_a)
                        v.tensor_sub(srow_b, srow_b, srow_c).then_inc(sV, 1)

                    nV += 1
                    d0_V = nV

                    @blk.scalar
                    def _(s):
                        s.wait_ge(sV, d0_V)
                        s.activation(srow_c, srow_b, AF.Ln, bias=cst_sb[0:1, 0:1])
                        s.activation(srow_c, srow_c, AF.Exp, scale=cst_sb[0:1, 1:2]).then_inc(sS, 1)

                    nS += 1
                    d0_S = nS

                    @blk.vector
                    def _(v):
                        v.wait_ge(sS, d0_S)
                        v.scalar_tensor_tensor(negs16[:], srow_a, -1.0, srow_c,
                                               ALU.mult, ALU.mult).then_inc(sV, 1)

                    nV += 1
                    d0_V2 = nV

                    @blk.tensor
                    def _(t):
                        t.wait_ge(sV, d0_V2)
                        for tc in range(TC):
                            t.matmul(pb[:, 2048 + tc * 512:2048 + (tc + 1) * 512],
                                     ones_1r[:], srow_c[:, tc * 512:(tc + 1) * 512],
                                     start=True, stop=True)
                        for kt in range(KT):
                            for nh in range(2):
                                mm = t.matmul(pb[0:1, nh * 512:(nh + 1) * 512], ones_p1[:],
                                              sf_sb[:, kt * D + nh * 512:kt * D + (nh + 1) * 512],
                                              start=(kt == 0), stop=(kt == KT - 1))
                        mm.then_inc(sT, 1)

                    nT += 1
                    d0_T2 = nT

                    @blk.vector
                    def _(v):
                        v.wait_ge(sT, d0_T2)
                        v.tensor_copy(cs_f[:], pb[0:1, 0:1024])
                        for tc in range(TC):
                            v.tensor_copy(rb_sb[:, tc * 512:(tc + 1) * 512],
                                          pb[:, 2048 + tc * 512:2048 + (tc + 1) * 512])
                        for od in range(OD):
                            vv = v.tensor_mul(x1l_sb[:, od * T:(od + 1) * T],
                                              x1_sb[:, od * T:(od + 1) * T], rb_sb[:])
                        vv.then_inc(sV, 1)

                    nV += 1
                    x1l_V = nV

                TE0, VE0, SE0, OE0 = nT, nV, nS, ndO
                TPE, VPE, SPE = 18, 10, 17
                if True:
                    @blk.tensor
                    def _(t):
                        t.wait_ge(sV, x1l_V)
                        t.wait_ge(sV, rs2_V)
                        for tc in range(TC):
                            base_s = SE0 + tc * SPE
                            base_v = VE0 + tc * VPE
                            if tc >= 1:
                                t.wait_ge(sV, VE0 + (tc - 1) * VPE + 2)
                            for kt in range(KT):
                                mm = t.matmul(pb[0:1, 2560:3072], rsum16[:, 2 * kt:2 * kt + 1],
                                              x1l_sb[:, kt * T + tc * 512:kt * T + (tc + 1) * 512],
                                              start=(kt == 0), stop=(kt == KT - 1))
                            mm.then_inc(sT, 1)
                            for op_ in range(4):
                                if op_ >= 2:
                                    t.wait_ge(sS, base_s + (op_ - 2) * 2 + 2)
                                elif tc >= 1:
                                    t.wait_ge(sS, SE0 + (tc - 1) * SPE + (op_ + 2) * 2 + 2)
                                for kt in range(KT):
                                    for j in range(2):
                                        od = 2 * op_ + j
                                        bslot = (op_ % 2) * 2 + j
                                        t.matmul(pb[:, bslot * 512:(bslot + 1) * 512],
                                                 sf_sb[:, kt * D + od * 128:kt * D + od * 128 + 128],
                                                 x1l_sb[:, kt * T + tc * 512:kt * T + (tc + 1) * 512],
                                                 start=(kt == 0), stop=False)
                                t.wait_ge(sV, base_v + 1)
                                for j in range(2):
                                    od = 2 * op_ + j
                                    bslot = (op_ % 2) * 2 + j
                                    t.matmul(pb[:, bslot * 512:(bslot + 1) * 512],
                                             cs_f[0:1, od * 128:(od + 1) * 128],
                                             negs16[0:1, tc * 512:(tc + 1) * 512],
                                             start=False, stop=False)
                                    t.matmul(pb[:, bslot * 512:(bslot + 1) * 512],
                                             ones_1r16[:], mrow16[0:1, tc * 512:(tc + 1) * 512],
                                             start=False, stop=True).then_inc(sT, 1)
                            for od in range(OD):
                                t.wait_ge(sS, base_s + 8 + od + 1)
                                t.matmul(pb[0:1, 2048:2560], ones_p1[:],
                                         sq2_t[:, (od % 2) * 512:(od % 2 + 1) * 512],
                                         start=(od == 0), stop=(od == OD - 1)).then_inc(sT, 1)
                            t.wait_ge(sS, base_s + SPE)
                            t.matmul(pb[:, 3072:3584], ones_1r[:],
                                     srow_c[:, tc * 512:(tc + 1) * 512],
                                     start=True, stop=True).then_inc(sT, 1)

                    nT += TC * TPE

                    @blk.scalar
                    def _(s):
                        for tc in range(TC):
                            base_t = TE0 + tc * TPE
                            for op_ in range(4):
                                for j in range(2):
                                    od = 2 * op_ + j
                                    bslot = (op_ % 2) * 2 + j
                                    s.wait_ge(sT, base_t + 1 + op_ * 2 + j + 1)
                                    if tc >= 1:
                                        s.wait_ge(sV, VE0 + (tc - 1) * VPE + 2 + od + 1)
                                    s.activation(y_t[:, od * 512:(od + 1) * 512],
                                                 pb[:, bslot * 512:(bslot + 1) * 512],
                                                 AF.Copy).then_inc(sS, 1)
                            for od in range(OD):
                                if od >= 2:
                                    s.wait_ge(sT, base_t + 1 + 8 + od - 1)
                                s.activation(sq2_t[:, (od % 2) * 512:(od % 2 + 1) * 512],
                                             y_t[:, od * 512:(od + 1) * 512],
                                             AF.Square).then_inc(sS, 1)
                            s.wait_ge(sT, base_t + 1 + 8 + 8)
                            s.activation(srow_a[:, tc * 512:(tc + 1) * 512], pb[0:1, 2048:2560], AF.Copy, scale=cst_sb[0:1, 2:3])
                            s.activation(srow_c[:, tc * 512:(tc + 1) * 512],
                                         srow_a[:, tc * 512:(tc + 1) * 512], AF.Ln, bias=cst_sb[0:1, 0:1])
                            s.activation(srow_c[:, tc * 512:(tc + 1) * 512],
                                         srow_c[:, tc * 512:(tc + 1) * 512], AF.Exp,
                                         scale=cst_sb[0:1, 1:2]).then_inc(sS, 1)

                    nS += TC * SPE

                    @blk.vector
                    def _(v):
                        for tc in range(TC):
                            base_t = TE0 + tc * TPE
                            v.wait_ge(sT, base_t + 1)
                            v.tensor_scalar(mrow16[0:1, tc * 512:(tc + 1) * 512],
                                            pb[0:1, 2560:3072], -1.0 / D, None, ALU.mult).then_inc(sV, 1)
                            v.wait_ge(sT, base_t + TPE)
                            v.tensor_copy(r4_sb, pb[:, 3072:3584]).then_inc(sV, 1)
                            for od in range(OD):
                                if tc * OD + od >= 4:
                                    v.wait_ge(dO, OE0 + 16 * (tc * OD + od - 3))
                                v.tensor_mul(ot_t[:, (od % 4) * 512:(od % 4 + 1) * 512],
                                             y_t[:, od * 512:(od + 1) * 512], r4_sb)
                                v.tensor_add(ot_t[:, (od % 4) * 512:(od % 4 + 1) * 512],
                                             ot_t[:, (od % 4) * 512:(od % 4 + 1) * 512],
                                             x1_sb[:, od * T + tc * 512:od * T + (tc + 1) * 512]).then_inc(sV, 1)

                    nV += TC * VPE

                    @blk.sync
                    def _(sync):
                        for tc in range(TC):
                            base_v = VE0 + tc * VPE
                            for od in range(OD):
                                sync.wait_ge(sV, base_v + 2 + od + 1)
                                sync.dma_start(
                                    out[od * 128:(od + 1) * 128, tc * 512:(tc + 1) * 512],
                                    ot_t[:, (od % 4) * 512:(od % 4 + 1) * 512]).then_inc(dO, 16)
                        sync.wait_ge(dO, OE0 + 16 * TC * OD)

                    ndO += 16 * TC * OD

    return nc


def kernel(L0_attn_x, L0_attn_Wq, L0_attn_Wk, L0_attn_Wv, L0_attn_Wo, L0_attn_Wb,
           L0_attn_Wfd, L0_attn_bfd, L0_attn_Wsd, L0_attn_bsd, L0_attn_Wg,
           L0_ffn_weight):
    x = np.asarray(L0_attn_x, np.float32)
    B, S, Dm = x.shape
    xT = np.ascontiguousarray(x.reshape(B * S, Dm).T)

    wT = {
        "wqT": np.ascontiguousarray(np.asarray(L0_attn_Wq, np.float32).T),
        "wkT": np.ascontiguousarray(np.asarray(L0_attn_Wk, np.float32).T),
        "wvT": np.ascontiguousarray(np.asarray(L0_attn_Wv, np.float32).T),
        "wgT": np.ascontiguousarray(np.asarray(L0_attn_Wg, np.float32).T),
        "woT": np.ascontiguousarray(np.asarray(L0_attn_Wo, np.float32).T),
        "wfT": np.ascontiguousarray(np.asarray(L0_ffn_weight, np.float32).T),
    }
    wbfs = np.concatenate([np.asarray(L0_attn_Wb, np.float32),
                           np.asarray(L0_attn_Wfd, np.float32),
                           np.asarray(L0_attn_Wsd, np.float32)], axis=0)
    wbfsT = np.ascontiguousarray(wbfs.T)
    bias48 = np.ascontiguousarray(np.concatenate(
        [np.zeros(16, np.float32),
         np.asarray(L0_attn_bfd, np.float32),
         np.asarray(L0_attn_bsd, np.float32)]).astype(np.float32)[:, None])

    ib16 = np.zeros((16, D), ml_dtypes.bfloat16)
    for hh in range(16):
        ib16[hh, hh * 64:(hh + 1) * 64] = 1.0
    cst = np.array([[EPS, -0.5, 1.0 / D, 0.0]], np.float32)

    nc = build_nc()
    in_maps = []
    for c in range(NCORE):
        sel = np.zeros((16, 8), np.float32)
        sel[:, c] = 1.0
        m = {"xT": np.ascontiguousarray(xT[:, c * T:(c + 1) * T]),
             "wbfsT": wbfsT, "bias48": bias48, "ib16": ib16, "sel16": sel,
             "cst": cst}
        m.update(wT)
        in_maps.append(m)

    res = run_bass_kernel_spmd(nc, in_maps, core_ids=list(range(NCORE)))
    outT = np.concatenate([res.results[c]["out"] for c in range(NCORE)], axis=1)
    return np.ascontiguousarray(outT.T).reshape(B, S, Dm)


if __name__ == "__main__":
    build_nc()
    print("built ok")


# version 21
# speedup vs baseline: 1.0802x; 1.0802x over previous
import contextlib

import numpy as np
import ml_dtypes

import concourse.bass as bass
import concourse.mybir as mybir
from concourse.bass_utils import run_bass_kernel_spmd

F32 = mybir.dt.float32
BF16 = mybir.dt.bfloat16
AF = mybir.ActivationFunctionType
ALU = mybir.AluOpType
AX = mybir.AxisListType

MIX = 0.05
C1 = 2.0 * (1.0 + MIX)
EPS = 1e-5
D = 1024
H = 16
HD = 64
T = 2048
OD = 8
KT = 8
TC = 4
NCH = 8
CT = 256
NCORE = 8


def build_nc():
    nc = bass.Bass("TRN2", target_bir_lowering=False, detect_race_conditions=False)

    xT = nc.dram_tensor("xT", [D, T], F32, kind="ExternalInput")
    wqT = nc.dram_tensor("wqT", [D, D], F32, kind="ExternalInput")
    wkT = nc.dram_tensor("wkT", [D, D], F32, kind="ExternalInput")
    wvT = nc.dram_tensor("wvT", [D, D], F32, kind="ExternalInput")
    wgT = nc.dram_tensor("wgT", [D, D], F32, kind="ExternalInput")
    woT = nc.dram_tensor("woT", [D, D], F32, kind="ExternalInput")
    wfT = nc.dram_tensor("wfT", [D, D], F32, kind="ExternalInput")
    wbfsT = nc.dram_tensor("wbfsT", [D, 48], F32, kind="ExternalInput")
    bias48 = nc.dram_tensor("bias48", [48, 1], F32, kind="ExternalInput")
    ib16 = nc.dram_tensor("ib16", [16, D], BF16, kind="ExternalInput")
    sel16 = nc.dram_tensor("sel16", [16, 8], F32, kind="ExternalInput")
    cst = nc.dram_tensor("cst", [1, 4], F32, kind="ExternalInput")
    out = nc.dram_tensor("out", [D, T], F32, kind="ExternalOutput")

    u_scr = nc.dram_tensor("u_scr", [D * T], BF16)
    zl_scr = nc.dram_tensor("zl_scr", [D * T], F32)
    qg_scr = nc.dram_tensor("qg_scr", [D * T], BF16)
    coef_scr = nc.dram_tensor("coef_scr", [2 * H * T], F32)
    wch_scr = nc.dram_tensor("wch_scr", [128 * 128], F32)
    rch_scr = nc.dram_tensor("rch_scr", [128 * 4], F32)
    corrl_scr = nc.dram_tensor("corrl_scr", [NCH * 32 * D], F32)
    rrow_scr = nc.dram_tensor("rrow_scr", [32 * T], F32)
    agin_scr = nc.dram_tensor("agin_scr", [1, 2112], F32)
    agout_scr = nc.dram_tensor("agout_scr", [8, 2112], F32)

    nT = nV = nS = nG = 0
    ndW = ndX = ndA = ndB = ndC = ndO = ncc = 0

    with contextlib.ExitStack() as _st:
        E = _st.enter_context
        ones_p1 = E(nc.sbuf_tensor("ones_p1", [128, 1], BF16))
        ones_p1f = E(nc.sbuf_tensor("ones_p1f", [128, 1], F32))
        ones_1r = E(nc.sbuf_tensor("ones_1r", [1, 128], F32))
        ones_1r16 = E(nc.sbuf_tensor("ones_1r16", [1, 128], BF16))
        sel_sb = E(nc.sbuf_tensor("sel_sb", [16, 8], F32))
        rsum16 = E(nc.sbuf_tensor("rsum16", [128, 2 * KT], BF16))
        rsumf = E(nc.sbuf_tensor("rsumf", [128, KT], F32))
        cst_sb = E(nc.sbuf_tensor("cst_sb", [1, 4], F32))
        zro_sb = E(nc.sbuf_tensor("zro_sb", [128, 128], F32))
        pb = E(nc.psum_tensor("pb", [128, 4096], F32))
        dW = E(nc.semaphore("dW"))
        dX = E(nc.semaphore("dX"))
        dA = E(nc.semaphore("dA"))
        dB = E(nc.semaphore("dB"))
        dCa = E(nc.semaphore("dCa"))
        dCb = E(nc.semaphore("dCb"))
        dCc = E(nc.semaphore("dCc"))
        dO = E(nc.semaphore("dO"))
        sT = E(nc.semaphore("sT"))
        sV = E(nc.semaphore("sV"))
        sS = E(nc.semaphore("sS"))
        cc = E(nc.semaphore("cc"))
        sG = E(nc.semaphore("sG"))
        dG = E(nc.semaphore("dG"))
        wA = E(nc.semaphore("wA"))
        wB = E(nc.semaphore("wB"))
        dCd = E(nc.semaphore("dCd"))
        dCe = E(nc.semaphore("dCe"))
        blk = E(nc.Block("ALL"))
        with contextlib.ExitStack() as _sa:
            EA = _sa.enter_context
            xl_sb = EA(nc.sbuf_tensor("xl_sb", [128, KT * T], BF16))
            ev_a = EA(nc.sbuf_tensor("ev_a", [128, 2 * 512], BF16, side="right"))
            ev_b = EA(nc.sbuf_tensor("ev_b", [128, 2 * 512], BF16, side="right"))
            negs16 = EA(nc.sbuf_tensor("negs16", [1, T], BF16, side="right"))
            cs_q = EA(nc.sbuf_tensor("cs_q", [1, D], BF16, side="right"))
            cs_g = EA(nc.sbuf_tensor("cs_g", [1, D], BF16, side="right"))
            qgt_sb = EA(nc.sbuf_tensor("qgt_sb", [128, 2 * 512], BF16, side="right"))
            _sa2 = contextlib.ExitStack()
            EA2 = _sa2.enter_context
            ib_sb = EA2(nc.sbuf_tensor("ib_sb", [16, D], BF16))
            rb_sb = EA2(nc.sbuf_tensor("rb_sb", [128, T], F32))
            srow_a = EA2(nc.sbuf_tensor("srow_a", [1, T], F32))[0:1, :]
            srow_b = EA2(nc.sbuf_tensor("srow_b", [1, T], F32))[0:1, :]
            srow_c = EA2(nc.sbuf_tensor("srow_c", [1, T], F32))[0:1, :]
            cs_k = EA2(nc.sbuf_tensor("cs_k", [1, D], BF16))
            cs_v = EA2(nc.sbuf_tensor("cs_v", [1, D], BF16))
            cs_b = EA2(nc.sbuf_tensor("cs_b", [1, 48], BF16))
            wbfs_sb = EA2(nc.sbuf_tensor("wbfs_sb", [128, KT * 48], BF16))
            b48_sb = EA2(nc.sbuf_tensor("b48_sb", [48, 1], F32))
            _sx = contextlib.ExitStack()
            x_sb = _sx.enter_context(nc.sbuf_tensor("x_sb", [128, OD * T], F32))
            xsq16 = _sx.enter_context(nc.sbuf_tensor("xsq16", [128, T], BF16))
            T0, V0, S0 = nT, nV, nS
            if True:
                @blk.sync
                def _(sync):
                    for od in range(OD):
                        sync.dma_start(x_sb[:, od * T:(od + 1) * T],
                                       xT[od * 128:(od + 1) * 128, :]).then_inc(dX, 16)

                ndX += 16 * OD

                @blk.gpsimd
                def _(g):
                    g.dma_start(cst_sb[:], cst[:]).then_inc(dW, 16)
                    g.memset(ones_p1[:], 1.0)
                    g.memset(ones_p1f[:], 1.0)
                    g.memset(ones_1r[:], 1.0)
                    g.memset(zro_sb[:], 0.0)
                    g.memset(ones_1r16[:], 1.0).then_inc(sG, 1)
                    g.dma_start(ib_sb[:], ib16[:]).then_inc(dW, 16)
                    g.dma_start(sel_sb[:], sel16[:]).then_inc(dW, 16)
                    g.dma_start(b48_sb[:], bias48[:]).then_inc(dW, 16)
                    for kt in range(KT):
                        g.dma_start(wbfs_sb[:, kt * 48:(kt + 1) * 48],
                                    wbfsT[kt * 128:(kt + 1) * 128, :]).then_inc(dW, 16)

                ndW += 16 * (4 + KT)
                nG += 1

                @blk.scalar
                def _(s):
                    s.wait_ge(dX, 16 * OD)
                    for kt in range(KT):
                        if kt >= 1:
                            s.wait_ge(sT, T0 + kt)
                        s.activation(xsq16[:, 0:T],
                                     x_sb[:, kt * T:(kt + 1) * T],
                                     AF.Square).then_inc(sS, 1)

                nS += KT

                @blk.tensor
                def _(t):
                    t.wait_ge(dX, 16 * OD)
                    t.wait_ge(sG, 1)
                    for kt in range(KT):
                        t.wait_ge(sS, S0 + kt + 1)
                        for tc in range(TC):
                            t.matmul(pb[0:1, tc * 512:(tc + 1) * 512], ones_p1f[:],
                                     x_sb[:, kt * T + tc * 512:kt * T + (tc + 1) * 512],
                                     start=(kt == 0), stop=(kt == KT - 1))
                        for tc in range(TC):
                            last = (kt == KT - 1) and (tc == TC - 1)
                            mm = t.matmul(
                                pb[0:1, 2048 + tc * 512:2048 + (tc + 1) * 512], ones_p1[:],
                                xsq16[:, tc * 512:(tc + 1) * 512],
                                start=(kt == 0), stop=(kt == KT - 1))
                        mm.then_inc(sT, 1)

                nT += KT

                @blk.vector
                def _(v):
                    v.wait_ge(sT, T0 + KT)
                    v.wait_ge(sG, 1)
                    v.tensor_scalar(srow_a, pb[0:1, 0:2048], 1.0 / D, None, ALU.mult)
                    v.tensor_scalar(srow_b, pb[0:1, 2048:4096], 1.0 / D, None, ALU.mult)
                    v.tensor_mul(srow_c, srow_a, srow_a)
                    v.tensor_sub(srow_b, srow_b, srow_c).then_inc(sV, 1)

                nV += 1

                @blk.scalar
                def _(s):
                    s.wait_ge(sV, V0 + 1)
                    s.wait_ge(dW, 16 * (4 + KT))
                    s.activation(srow_c, srow_b, AF.Ln, bias=cst_sb[0:1, 0:1])
                    s.activation(srow_c, srow_c, AF.Exp, scale=cst_sb[0:1, 1:2]).then_inc(sS, 1)

                nS += 1

                @blk.vector
                def _(v):
                    v.wait_ge(sS, S0 + KT + 1)
                    v.scalar_tensor_tensor(negs16[:], srow_a, -1.0, srow_c,
                                           ALU.mult, ALU.mult).then_inc(sV, 1)

                nV += 1

                @blk.tensor
                def _(t):
                    t.wait_ge(sV, V0 + 2)
                    for tc in range(TC):
                        mm = t.matmul(pb[:, 2048 + tc * 512:2048 + (tc + 1) * 512],
                                      ones_1r[:], srow_c[:, tc * 512:(tc + 1) * 512],
                                      start=True, stop=True)
                    mm.then_inc(sT, 1)

                nT += 1

                @blk.vector
                def _(v):
                    v.wait_ge(sT, T0 + KT + 1)
                    for tc in range(TC):
                        v.tensor_copy(rb_sb[:, tc * 512:(tc + 1) * 512],
                                      pb[:, 2048 + tc * 512:2048 + (tc + 1) * 512])
                    for od in range(OD):
                        vv = v.tensor_mul(xl_sb[:, od * T:(od + 1) * T],
                                          x_sb[:, od * T:(od + 1) * T], rb_sb[:])
                    vv.then_inc(sV, 1)

                nV += 1

            _sx.close()
            wp_sb = EA(nc.sbuf_tensor("wp_sb", [128, 2 * KT * D], BF16, side="right"))
            bfs_sb = EA2(nc.sbuf_tensor("bfs_sb", [48, T], F32))
            f_sb = EA2(nc.sbuf_tensor("f_sb", [16, T], F32))
            s_sb = EA2(nc.sbuf_tensor("s_sb", [16, T], F32))
            brow16 = EA2(nc.sbuf_tensor("brow16", [16, T], BF16))
            trow = EA2(nc.sbuf_tensor("trow", [16, T], F32))
            kv_t = EA2(nc.sbuf_tensor("kv_t", [128, 2 * 512], F32))
            u_t = EA2(nc.sbuf_tensor("u_t", [128, 2 * 512], BF16))

            wstg = nc.alloc_sbuf_tensor_at("wstg", [128, 2 * D], F32, offset=160512)
            cast_S = nS
            if True:
                @blk.sync
                def _(sync):
                    for i in range(2 * KT):
                        kt, is_v = i // 2, i % 2
                        if i >= 2:
                            sync.wait_ge(sS, cast_S + i - 1)
                        sync.dma_start(wstg[:, (i % 2) * D:(i % 2 + 1) * D],
                                       (wvT if is_v else wkT)[kt * 128:(kt + 1) * 128, :]
                                       ).then_inc(wA if i % 2 == 0 else wB, 16)

                @blk.scalar
                def _(s):
                    for i in range(2 * KT):
                        kt, is_v = i // 2, i % 2
                        s.wait_ge(wA if i % 2 == 0 else wB, 16 * (i // 2 + 1))
                        s.activation(wp_sb[:, (KT * is_v + kt) * D:(KT * is_v + kt + 1) * D],
                                     wstg[:, (i % 2) * D:(i % 2 + 1) * D],
                                     AF.Copy).then_inc(sS, 1)

            nwA = nwB = 16 * KT
            nS += 2 * KT

            T1, V1, S1 = nT, nV, nS
            if True:
                @blk.tensor
                def _(t):
                    t.wait_ge(sS, cast_S + 2 * KT)
                    t.wait_ge(sV, V1)
                    for kt in range(KT):
                        for nh in range(2):
                            t.matmul(pb[0:1, nh * 512:(nh + 1) * 512], ones_p1[:],
                                     wp_sb[:, kt * D + nh * 512:kt * D + (nh + 1) * 512],
                                     start=(kt == 0), stop=(kt == KT - 1))
                            t.matmul(pb[0:1, 1024 + nh * 512:1536 + nh * 512], ones_p1[:],
                                     wp_sb[:, (KT + kt) * D + nh * 512:(KT + kt) * D + (nh + 1) * 512],
                                     start=(kt == 0), stop=(kt == KT - 1))
                        mm = t.matmul(pb[0:1, 2048:2096], ones_p1[:],
                                      wbfs_sb[:, kt * 48:(kt + 1) * 48],
                                      start=(kt == 0), stop=(kt == KT - 1))
                    mm.then_inc(sT, 1)

                nT += 1

                @blk.vector
                def _(v):
                    v.wait_ge(sT, T1 + 1)
                    v.tensor_copy(cs_k[:], pb[0:1, 0:1024])
                    v.tensor_copy(cs_v[:], pb[0:1, 1024:2048])
                    v.tensor_copy(cs_b[:], pb[0:1, 2048:2096]).then_inc(sV, 1)

                nV += 1

                @blk.tensor
                def _(t):
                    t.wait_ge(sV, V1 + 1)
                    for kt in range(KT):
                        for tc in range(TC):
                            t.matmul(pb[0:48, tc * 512:(tc + 1) * 512],
                                     wbfs_sb[:, kt * 48:(kt + 1) * 48],
                                     xl_sb[:, kt * T + tc * 512:kt * T + (tc + 1) * 512],
                                     start=(kt == 0), stop=False)
                    for tc in range(TC):
                        t.matmul(pb[0:48, tc * 512:(tc + 1) * 512], cs_b[:],
                                 negs16[0:1, tc * 512:(tc + 1) * 512],
                                 start=False, stop=True).then_inc(sT, 1)

                nT += TC

                @blk.scalar
                def _(s):
                    for tc in range(TC):
                        s.wait_ge(sT, T1 + 1 + tc + 1)
                        s.activation(bfs_sb[0:48, tc * 512:(tc + 1) * 512],
                                     pb[0:48, tc * 512:(tc + 1) * 512],
                                     AF.Sigmoid, bias=b48_sb[:]).then_inc(sS, 1)

                nS += TC

                @blk.sync
                def _(sync):
                    sync.wait_ge(sS, S1 + TC)
                    sync.dma_start(f_sb[:], bfs_sb[16:32, :]).then_inc(dA, 16)
                    sync.dma_start(s_sb[:], bfs_sb[32:48, :]).then_inc(dA, 16)

                @blk.vector
                def _(v):
                    v.wait_ge(dA, ndA + 32)
                    v.tensor_copy(brow16[:], bfs_sb[0:16, :])
                    v.tensor_add(trow[:], f_sb[:], s_sb[:])
                    v.tensor_sub(s_sb[:], f_sb[:], s_sb[:]).then_inc(sV, 1)

                nV += 1

                @blk.sync
                def _(sync):
                    sync.wait_ge(sV, V1 + 2)
                    sync.dma_start(bass.AP(coef_scr, 0, [[T, 16], [1, T]]),
                                   trow[:]).then_inc(dA, 16)
                    sync.dma_start(bass.AP(coef_scr, H * T, [[T, 16], [1, T]]),
                                   s_sb[:]).then_inc(dA, 16)

                ndA += 64

            T2, V2, S2, A2 = nT, nV, nS, ndA
            if True:
                @blk.tensor
                def _(t):
                    t.wait_ge(sV, V2)
                    for od in range(OD):
                        if od > 0:
                            t.wait_ge(sV, V2 + od * 8)
                        for kt in range(KT):
                            for tc in range(TC):
                                t.matmul(pb[:, tc * 512:(tc + 1) * 512],
                                         wp_sb[:, kt * D + od * 128:kt * D + od * 128 + 128],
                                         xl_sb[:, kt * T + tc * 512:kt * T + (tc + 1) * 512],
                                         start=(kt == 0), stop=False)
                            for tc in range(TC):
                                t.matmul(pb[:, 2048 + tc * 512:2048 + (tc + 1) * 512],
                                         wp_sb[:, (KT + kt) * D + od * 128:(KT + kt) * D + od * 128 + 128],
                                         xl_sb[:, kt * T + tc * 512:kt * T + (tc + 1) * 512],
                                         start=(kt == 0), stop=False)
                        for tc in range(TC):
                            t.matmul(pb[:, tc * 512:(tc + 1) * 512],
                                     cs_k[0:1, od * 128:(od + 1) * 128],
                                     negs16[0:1, tc * 512:(tc + 1) * 512],
                                     start=False, stop=True)
                            t.matmul(pb[:, 2048 + tc * 512:2048 + (tc + 1) * 512],
                                     cs_v[0:1, od * 128:(od + 1) * 128],
                                     negs16[0:1, tc * 512:(tc + 1) * 512],
                                     start=False, stop=True).then_inc(sT, 1)
                        for tc in range(TC):
                            t.wait_ge(sS, S2 + od * 8 + 2 * (tc + 1))
                            t.matmul(pb[:, tc * 512:(tc + 1) * 512],
                                     ib_sb[:, od * 128:(od + 1) * 128],
                                     brow16[0:16, tc * 512:(tc + 1) * 512],
                                     start=True, stop=True).then_inc(sT, 1)

                nT += OD * 8

                @blk.scalar
                def _(s):
                    for od in range(OD):
                        for tc in range(TC):
                            s.wait_ge(sT, T2 + od * 8 + tc + 1)
                            if tc >= 2:
                                s.wait_ge(sV, V2 + od * 8 + (tc - 2) * 2 + 1)
                            s.activation(ev_a[:, (tc % 2) * 512:(tc % 2 + 1) * 512],
                                         pb[:, tc * 512:(tc + 1) * 512], AF.Silu)
                            s.activation(ev_b[:, (tc % 2) * 512:(tc % 2 + 1) * 512],
                                         pb[:, 2048 + tc * 512:2048 + (tc + 1) * 512],
                                         AF.Silu).then_inc(sS, 2)

                nS += OD * 8

                @blk.vector
                def _(v):
                    for od in range(OD):
                        for tc in range(TC):
                            v.wait_ge(sS, S2 + od * 8 + 2 * (tc + 1))
                            v.tensor_mul(kv_t[:, (tc % 2) * 512:(tc % 2 + 1) * 512],
                                         ev_a[:, (tc % 2) * 512:(tc % 2 + 1) * 512],
                                         ev_b[:, (tc % 2) * 512:(tc % 2 + 1) * 512]).then_inc(sV, 1)
                            v.wait_ge(sT, T2 + od * 8 + TC + tc + 1)
                            if tc >= 2:
                                v.wait_ge(dA, A2 + 16 * (od * 4 + tc - 1))
                            v.tensor_mul(u_t[:, (tc % 2) * 512:(tc % 2 + 1) * 512],
                                         kv_t[:, (tc % 2) * 512:(tc % 2 + 1) * 512],
                                         pb[:, tc * 512:(tc + 1) * 512]).then_inc(sV, 1)

                nV += OD * 8

                @blk.sync
                def _(sync):
                    for od in range(OD):
                        for tc in range(TC):
                            sync.wait_ge(sV, V2 + od * 8 + tc * 2 + 2)
                            sync.dma_start(
                                bass.AP(u_scr, od * 128 * T + tc * 512, [[T, 128], [1, 512]]),
                                u_t[:, (tc % 2) * 512:(tc % 2 + 1) * 512]).then_inc(dA, 16)

                ndA += 16 * OD * TC

            _sa2.close()
            a2_dA, a2_sT, a2_sV, a2_sS = ndA, nT, nV, nS

            _sb0 = contextlib.ExitStack()
            EB0 = _sb0.enter_context
            us_sb = EB0(nc.sbuf_tensor("us_sb", [128, HD * CT], BF16))
            ph_sb = EB0(nc.sbuf_tensor("ph_sb", [128, HD * CT], F32))
            co_sb = EB0(nc.sbuf_tensor("co_sb", [128, 4 * CT], F32, side="right"))
            ks_sb = EB0(nc.sbuf_tensor("ks_sb", [128, 4 * CT], F32, side="right"))
            qs_sb = EB0(nc.sbuf_tensor("qs_sb", [128, 2 * HD], F32, side="right"))
            tmp_sb = EB0(nc.sbuf_tensor("tmp_sb", [128, 128], F32, side="right"))

            if True:
                @blk.sync
                def _(sync):
                    sync.wait_ge(dA, a2_dA)
                    sync.wait_ge(sV, a2_sV)
                    sync.wait_ge(sS, a2_sS)
                    sync.wait_ge(sT, a2_sT)
                    for h in range(H):
                        sync.dma_start(
                            bass.AP(us_sb, h * 8 * (HD * CT), [[HD * CT, 8], [CT, HD], [1, CT]]),
                            bass.AP(u_scr, h * HD * T, [[CT, 8], [T, HD], [1, CT]])).then_inc(dB, 16)
                    for e in range(2):
                        sync.dma_start(
                            bass.AP(co_sb, e * CT, [[4 * CT, 128], [1, CT]]),
                            bass.AP(coef_scr, e * H * T, [[CT, 128], [1, CT]])).then_inc(dB, 16)

                ndB += 16 * 18
                b0_dB = ndB

                @blk.vector
                def _(v):
                    v.wait_ge(dB, b0_dB)
                    v.tensor_scalar(co_sb[:, 3 * CT:4 * CT], co_sb[:, 0:CT],
                                    (1 - MIX) / 2, None, ALU.mult)
                    v.tensor_scalar(co_sb[:, 2 * CT:3 * CT], co_sb[:, CT:2 * CT],
                                    (1 - MIX) / 2, None, ALU.mult)
                    v.tensor_scalar(co_sb[:, 0:CT], co_sb[:, 0:CT],
                                    (1 + MIX) / 2, None, ALU.mult)
                    v.tensor_scalar(co_sb[:, CT:2 * CT], co_sb[:, CT:2 * CT],
                                    (1 + MIX) / 2, None, ALU.mult)
                    v.memset(qs_sb[:, 0:HD], 0.0)
                    v.tensor_copy(bass.AP(ph_sb, 0, [[HD * CT, 128], [CT, HD]]),
                                  bass.AP(us_sb, 0, [[HD * CT, 128], [CT, HD]]))
                    for t_ in range(1, CT):
                        pr = bass.AP(ph_sb, t_ - 1, [[HD * CT, 128], [CT, HD]])
                        ut_ = bass.AP(us_sb, t_, [[HD * CT, 128], [CT, HD]])
                        qprev = qs_sb[:, ((t_ - 1) % 2) * HD:((t_ - 1) % 2 + 1) * HD]
                        qcur = qs_sb[:, (t_ % 2) * HD:(t_ % 2 + 1) * HD]
                        v.scalar_tensor_tensor(tmp_sb[:, 0:HD], qprev,
                                               co_sb[:, CT + t_:CT + t_ + 1], ut_,
                                               ALU.mult, ALU.add)
                        v.scalar_tensor_tensor(
                            bass.AP(ph_sb, t_, [[HD * CT, 128], [CT, HD]]), pr,
                            co_sb[:, t_:t_ + 1], tmp_sb[:, 0:HD], ALU.mult, ALU.add)
                        v.tensor_scalar(tmp_sb[:, HD:2 * HD], pr,
                                        co_sb[:, 2 * CT + t_:2 * CT + t_ + 1], None, ALU.mult)
                        v.scalar_tensor_tensor(qcur, qprev,
                                               co_sb[:, 3 * CT + t_:3 * CT + t_ + 1],
                                               tmp_sb[:, HD:2 * HD], ALU.mult, ALU.add)
                    v.tensor_copy(qs_sb[:, 0:HD],
                                  bass.AP(ph_sb, CT - 1, [[HD * CT, 128], [CT, HD]]))
                    src_, dst = co_sb, ks_sb
                    for rnd in range(8):
                        s_ = 1 << rnd
                        n = CT - s_
                        for e in range(4):
                            v.tensor_copy(dst[:, e * CT:e * CT + s_], src_[:, e * CT:e * CT + s_])
                        a00 = src_[:, 0 * CT + s_:1 * CT]
                        a01 = src_[:, 1 * CT + s_:2 * CT]
                        a10 = src_[:, 2 * CT + s_:3 * CT]
                        a11 = src_[:, 3 * CT + s_:4 * CT]
                        b00 = src_[:, 0 * CT:0 * CT + n]
                        b01 = src_[:, 1 * CT:1 * CT + n]
                        b10 = src_[:, 2 * CT:2 * CT + n]
                        b11 = src_[:, 3 * CT:3 * CT + n]
                        d00 = dst[:, 0 * CT + s_:1 * CT]
                        d01 = dst[:, 1 * CT + s_:2 * CT]
                        d10 = dst[:, 2 * CT + s_:3 * CT]
                        d11 = dst[:, 3 * CT + s_:4 * CT]
                        v.tensor_mul(d00, a00, b00)
                        v.tensor_mul(d01, a01, b10)
                        v.tensor_add(d00, d00, d01)
                        v.tensor_mul(d01, a00, b01)
                        v.tensor_mul(d10, a01, b11)
                        v.tensor_add(d01, d01, d10)
                        v.tensor_mul(d10, a10, b00)
                        v.tensor_mul(d11, a11, b10)
                        v.tensor_add(d10, d10, d11)
                        v.tensor_mul(d11, a10, b01)
                        for off in range(0, n, 128):
                            w = min(128, n - off)
                            v.tensor_mul(tmp_sb[:, 0:w], a11[:, off:off + w], b11[:, off:off + w])
                            v.tensor_add(dst[:, 3 * CT + s_ + off:3 * CT + s_ + off + w],
                                         dst[:, 3 * CT + s_ + off:3 * CT + s_ + off + w],
                                         tmp_sb[:, 0:w])
                        src_, dst = dst, src_
                    v.tensor_copy(tmp_sb[:, 0:1], co_sb[:, 0:1]).then_inc(sV, 1)

                nV += 1
                ks_V = nV

            T3, S3 = nT, nS
            G3 = nG
            if True:
                @blk.gpsimd
                def _(g):
                    g.wait_ge(sT, T3)
                    for kt in range(KT):
                        g.dma_start(wp_sb[:, kt * D:(kt + 1) * D],
                                    wqT[kt * 128:(kt + 1) * 128, :]).then_inc(dW, 16)
                        g.dma_start(wp_sb[:, (KT + kt) * D:(KT + kt + 1) * D],
                                    wgT[kt * 128:(kt + 1) * 128, :]).then_inc(dW, 16)
                    i = 0
                    for od in range(OD):
                        for tc in range(TC):
                            half, j = tc // 2, tc % 2
                            g.wait_ge(sS, S3 + 1 + od * 8 + half * 4 + 2 + j + 1)
                            if i >= 2:
                                g.wait_ge(dG, 16 * (i - 1))
                            g.tensor_mul(qgt_sb[:, (i % 2) * 512:(i % 2 + 1) * 512],
                                         ev_a[:, j * 512:(j + 1) * 512],
                                         ev_b[:, j * 512:(j + 1) * 512]).then_inc(sG, 1)
                            g.dma_start(
                                bass.AP(qg_scr, od * 128 * T + tc * 512, [[T, 128], [1, 512]]),
                                qgt_sb[:, (i % 2) * 512:(i % 2 + 1) * 512]).then_inc(dG, 16)
                            i += 1

                qg_dW = ndW + 16 * 2 * KT
                ndW = qg_dW
                nG += OD * TC
                ndG_a3 = 16 * OD * TC

                @blk.tensor
                def _(t):
                    t.wait_ge(dW, qg_dW)
                    t.wait_ge(sV, a2_sV)
                    for kt in range(KT):
                        for nh in range(2):
                            t.matmul(pb[0:1, nh * 512:(nh + 1) * 512], ones_p1[:],
                                     wp_sb[:, kt * D + nh * 512:kt * D + (nh + 1) * 512],
                                     start=(kt == 0), stop=(kt == KT - 1))
                            mm = t.matmul(pb[0:1, 1024 + nh * 512:1536 + nh * 512], ones_p1[:],
                                     wp_sb[:, (KT + kt) * D + nh * 512:(KT + kt) * D + (nh + 1) * 512],
                                     start=(kt == 0), stop=(kt == KT - 1))
                    mm.then_inc(sT, 1)
                    t.wait_ge(sS, S3 + 1)
                    for od in range(OD):
                        if od > 0:
                            t.wait_ge(sS, S3 + 1 + od * 8)
                        for kt in range(KT):
                            for tc in range(TC):
                                t.matmul(pb[:, tc * 512:(tc + 1) * 512],
                                         wp_sb[:, kt * D + od * 128:kt * D + od * 128 + 128],
                                         xl_sb[:, kt * T + tc * 512:kt * T + (tc + 1) * 512],
                                         start=(kt == 0), stop=False)
                            for tc in range(TC):
                                t.matmul(pb[:, 2048 + tc * 512:2048 + (tc + 1) * 512],
                                         wp_sb[:, (KT + kt) * D + od * 128:(KT + kt) * D + od * 128 + 128],
                                         xl_sb[:, kt * T + tc * 512:kt * T + (tc + 1) * 512],
                                         start=(kt == 0), stop=False)
                        for tc in range(TC):
                            t.matmul(pb[:, tc * 512:(tc + 1) * 512],
                                     cs_q[0:1, od * 128:(od + 1) * 128],
                                     negs16[0:1, tc * 512:(tc + 1) * 512],
                                     start=False, stop=True)
                            t.matmul(pb[:, 2048 + tc * 512:2048 + (tc + 1) * 512],
                                     cs_g[0:1, od * 128:(od + 1) * 128],
                                     negs16[0:1, tc * 512:(tc + 1) * 512],
                                     start=False, stop=True).then_inc(sT, 1)

                nT += 1 + OD * TC

                @blk.scalar
                def _(s):
                    s.wait_ge(sT, T3 + 1)
                    s.activation(cs_q[:], pb[0:1, 0:1024], AF.Copy)
                    s.activation(cs_g[:], pb[0:1, 1024:2048], AF.Copy).then_inc(sS, 1)
                    for od in range(OD):
                        for half in range(2):
                            for j in range(2):
                                tc = half * 2 + j
                                s.wait_ge(sT, T3 + 1 + od * 4 + tc + 1)
                                prev = od * 4 + tc - 2
                                if prev >= 0:
                                    s.wait_ge(sG, G3 + prev + 1)
                                s.activation(ev_a[:, j * 512:(j + 1) * 512],
                                             pb[:, tc * 512:(tc + 1) * 512],
                                             AF.Silu).then_inc(sS, 1)
                            for j in range(2):
                                tc = half * 2 + j
                                s.activation(ev_b[:, j * 512:(j + 1) * 512],
                                             pb[:, 2048 + tc * 512:2048 + (tc + 1) * 512],
                                             AF.Sigmoid).then_inc(sS, 1)

                nS += 1 + OD * 8

            a3_sT, a3_sS, a3_sG = nT, nS, nG


            _sb1 = contextlib.ExitStack()
            EB1 = _sb1.enter_context
            wch_sb = EB1(nc.sbuf_tensor("wch_sb", [16, 8 * 128], F32))
            rch_sb = EB1(nc.sbuf_tensor("rch_sb", [16, 32], F32))
            ech_sb = EB1(nc.sbuf_tensor("ech_sb", [16, 9 * 128], F32))
            rp_sb = EB1(nc.sbuf_tensor("rp_sb", [16, 8], F32))
            agw_sb = EB1(nc.sbuf_tensor("agw_sb", [16, 8 * 128], F32))
            agr_sb = EB1(nc.sbuf_tensor("agr_sb", [16, 32], F32))
            ent_sb = EB1(nc.sbuf_tensor("ent_sb", [16, 128], F32))
            ecc_sb = ech_sb
            eg_sb = agw_sb

            if True:
                @blk.sync
                def _(sync):
                    sync.wait_ge(sV, ks_V)
                    sync.wait_ge(sT, a3_sT)
                    sync.wait_ge(sS, a3_sS)
                    sync.wait_ge(sG, a3_sG)
                    sync.dma_start(bass.AP(wch_scr, 0, [[128, 128], [1, 128]]),
                                   qs_sb[:]).then_inc(dB, 16)
                    sync.dma_start(bass.AP(wch_scr, 0, [[128, 128], [1, 64]]),
                                   qs_sb[:, 0:64]).then_inc(dB, 16)
                    for e in range(4):
                        sync.dma_start(bass.AP(rch_scr, e * 128, [[1, 128]]),
                                       co_sb[:, e * CT + CT - 1:(e * CT) + CT]).then_inc(dB, 16)
                    sync.wait_ge(dB, b0_dB + 16 * 6)
                    for h in range(H):
                        sync.dma_start(
                            bass.AP(rrow_scr, h * 2 * T, [[CT, 8], [T, 2], [1, CT]]),
                            bass.AP(co_sb, (h * 8) * (4 * CT),
                                    [[4 * CT, 8], [CT, 2], [1, CT]])).then_inc(dB, 16)
                    sync.dma_start(wch_sb[:], bass.AP(wch_scr, 0, [[1024, 16], [1, 1024]])).then_inc(dB, 16)
                    sync.dma_start(rch_sb[:], bass.AP(rch_scr, 0, [[8, 16], [128, 4], [1, 8]])).then_inc(dB, 16)

                ndB += 16 * (6 + 16 + 2)
                b1_dB = ndB

                @blk.vector
                def _(v):
                    v.scalar_tensor_tensor(ph_sb[:], us_sb[:], -1.0, ph_sb[:],
                                           ALU.mult, ALU.add)
                    v.nop()
                    v.memset(tmp_sb[:, 0:1], 0.0).then_inc(sV, 1)
                    v.nop()
                    v.wait_ge(dB, b1_dB)
                    v.memset(ech_sb[:, 0:128], 0.0)
                    for c in range(1, 9):
                        pP = ech_sb[:, (c - 1) * 128:(c - 1) * 128 + 64]
                        pQ = ech_sb[:, (c - 1) * 128 + 64:c * 128]
                        wP = wch_sb[:, (c - 1) * 128:(c - 1) * 128 + 64]
                        wQ = wch_sb[:, (c - 1) * 128 + 64:c * 128]
                        r0 = rch_sb[:, 0 * 8 + (c - 1):0 * 8 + c]
                        r1 = rch_sb[:, 1 * 8 + (c - 1):1 * 8 + c]
                        r2_ = rch_sb[:, 2 * 8 + (c - 1):2 * 8 + c]
                        r3 = rch_sb[:, 3 * 8 + (c - 1):3 * 8 + c]
                        v.scalar_tensor_tensor(tmp_sb[0:16, 0:64], pQ, r1, wP, ALU.mult, ALU.add)
                        v.scalar_tensor_tensor(ech_sb[:, c * 128:c * 128 + 64], pP, r0,
                                               tmp_sb[0:16, 0:64], ALU.mult, ALU.add)
                        v.scalar_tensor_tensor(tmp_sb[0:16, 64:128], pQ, r3, wQ, ALU.mult, ALU.add)
                        v.scalar_tensor_tensor(ech_sb[:, c * 128 + 64:(c + 1) * 128], pP, r2_,
                                               tmp_sb[0:16, 64:128], ALU.mult, ALU.add)
                    v.memset(rp_sb[:], 0.0)
                    v.tensor_scalar(rp_sb[:, 0:1], ones_p1f[0:16, 0:1], 1.0, None, ALU.mult)
                    v.tensor_scalar(rp_sb[:, 3:4], ones_p1f[0:16, 0:1], 1.0, None, ALU.mult)
                    for c in range(8):
                        pi = (c % 2) * 4
                        po = ((c + 1) % 2) * 4
                        r0 = rch_sb[:, 0 * 8 + c:0 * 8 + c + 1]
                        r1 = rch_sb[:, 1 * 8 + c:1 * 8 + c + 1]
                        r2_ = rch_sb[:, 2 * 8 + c:2 * 8 + c + 1]
                        r3 = rch_sb[:, 3 * 8 + c:3 * 8 + c + 1]
                        v.tensor_scalar(tmp_sb[0:16, 0:1], rp_sb[:, pi + 2:pi + 3], r1, None, ALU.mult)
                        v.scalar_tensor_tensor(rp_sb[:, po + 0:po + 1], rp_sb[:, pi + 0:pi + 1],
                                               r0, tmp_sb[0:16, 0:1], ALU.mult, ALU.add)
                        v.tensor_scalar(tmp_sb[0:16, 0:1], rp_sb[:, pi + 3:pi + 4], r1, None, ALU.mult)
                        v.scalar_tensor_tensor(rp_sb[:, po + 1:po + 2], rp_sb[:, pi + 1:pi + 2],
                                               r0, tmp_sb[0:16, 0:1], ALU.mult, ALU.add)
                        v.tensor_scalar(tmp_sb[0:16, 0:1], rp_sb[:, pi + 2:pi + 3], r3, None, ALU.mult)
                        v.scalar_tensor_tensor(rp_sb[:, po + 2:po + 3], rp_sb[:, pi + 0:pi + 1],
                                               r2_, tmp_sb[0:16, 0:1], ALU.mult, ALU.add)
                        v.tensor_scalar(tmp_sb[0:16, 0:1], rp_sb[:, pi + 3:pi + 4], r3, None, ALU.mult)
                        v.scalar_tensor_tensor(rp_sb[:, po + 3:po + 4], rp_sb[:, pi + 1:pi + 2],
                                               r2_, tmp_sb[0:16, 0:1], ALU.mult, ALU.add)
                    v.tensor_copy(tmp_sb[0:16, 0:1], rp_sb[:, 0:1]).then_inc(sV, 1)

                nV += 1
                zloc_V = nV
                nV += 1
                chain_V = nV

                @blk.gpsimd
                def _(g):
                    g.wait_ge(sV, chain_V)
                    g.dma_start(bass.AP(agin_scr, 0, [[128, 16], [1, 128]]),
                                ech_sb[:, 8 * 128:9 * 128]).then_inc(dG, 16)
                    g.dma_start(bass.AP(agin_scr, 2048, [[4, 16], [1, 4]]),
                                rp_sb[:, 0:4]).then_inc(dG, 16)
                    g.wait_ge(dG, ndG_a3 + 32)
                    g.collective_compute(
                        "AllGather", ALU.bypass, [[0, 1, 2, 3, 4, 5, 6, 7]],
                        [agin_scr.ap().opt()], [agout_scr.ap().opt()]).then_inc(cc, 1)
                    g.wait_ge(cc, 1)
                    g.dma_start(agw_sb[:],
                                bass.AP(agout_scr, 0, [[128, 16], [2112, 8], [1, 128]])).then_inc(dG, 16)
                    g.dma_start(agr_sb[:],
                                bass.AP(agout_scr, 2048, [[4, 16], [2112, 8], [1, 4]])).then_inc(dG, 16)

                ncc += 1

                @blk.vector
                def _(v):
                    v.wait_ge(dG, ndG_a3 + 64)
                    v.memset(ecc_sb[:, 0:128], 0.0)
                    for k in range(1, 8):
                        if k == 4:
                            v.memset(ecc_sb[:, 4 * 128:5 * 128], 0.0)
                            continue
                        pP = ecc_sb[:, (k - 1) * 128:(k - 1) * 128 + 64]
                        pQ = ecc_sb[:, (k - 1) * 128 + 64:k * 128]
                        wP = agw_sb[:, (k - 1) * 128:(k - 1) * 128 + 64]
                        wQ = agw_sb[:, (k - 1) * 128 + 64:k * 128]
                        r0 = agr_sb[:, (k - 1) * 4 + 0:(k - 1) * 4 + 1]
                        r1 = agr_sb[:, (k - 1) * 4 + 1:(k - 1) * 4 + 2]
                        r2_ = agr_sb[:, (k - 1) * 4 + 2:(k - 1) * 4 + 3]
                        r3 = agr_sb[:, (k - 1) * 4 + 3:(k - 1) * 4 + 4]
                        v.scalar_tensor_tensor(tmp_sb[0:16, 0:64], pQ, r1, wP, ALU.mult, ALU.add)
                        v.scalar_tensor_tensor(ecc_sb[:, k * 128:k * 128 + 64], pP, r0,
                                               tmp_sb[0:16, 0:64], ALU.mult, ALU.add)
                        v.scalar_tensor_tensor(tmp_sb[0:16, 64:128], pQ, r3, wQ, ALU.mult, ALU.add)
                        v.scalar_tensor_tensor(ecc_sb[:, k * 128 + 64:(k + 1) * 128], pP, r2_,
                                               tmp_sb[0:16, 64:128], ALU.mult, ALU.add)
                    v.memset(ent_sb[:], 0.0)
                    for k in range(8):
                        v.scalar_tensor_tensor(ent_sb[:], ecc_sb[:, k * 128:(k + 1) * 128],
                                               sel_sb[:, k:k + 1], ent_sb[:], ALU.mult, ALU.add)
                    v.tensor_copy(eg_sb[:, 0:128], ent_sb[:])
                    for c in range(1, 8):
                        pP = eg_sb[:, (c - 1) * 128:(c - 1) * 128 + 64]
                        pQ = eg_sb[:, (c - 1) * 128 + 64:c * 128]
                        wP = wch_sb[:, (c - 1) * 128:(c - 1) * 128 + 64]
                        wQ = wch_sb[:, (c - 1) * 128 + 64:c * 128]
                        r0 = rch_sb[:, 0 * 8 + (c - 1):0 * 8 + c]
                        r1 = rch_sb[:, 1 * 8 + (c - 1):1 * 8 + c]
                        r2_ = rch_sb[:, 2 * 8 + (c - 1):2 * 8 + c]
                        r3 = rch_sb[:, 3 * 8 + (c - 1):3 * 8 + c]
                        v.scalar_tensor_tensor(tmp_sb[0:16, 0:64], pQ, r1, wP, ALU.mult, ALU.add)
                        v.scalar_tensor_tensor(eg_sb[:, c * 128:c * 128 + 64], pP, r0,
                                               tmp_sb[0:16, 0:64], ALU.mult, ALU.add)
                        v.scalar_tensor_tensor(tmp_sb[0:16, 64:128], pQ, r3, wQ, ALU.mult, ALU.add)
                        v.scalar_tensor_tensor(eg_sb[:, c * 128 + 64:(c + 1) * 128], pP, r2_,
                                               tmp_sb[0:16, 64:128], ALU.mult, ALU.add)
                    v.nop()
                    v.memset(tmp_sb[:, 1:2], 0.0).then_inc(sV, 1)
                    v.nop()

                nV += 1
                eg_V = nV

                @blk.sync
                def _(sync):
                    sync.wait_ge(sV, zloc_V)
                    for h in range(H // 2):
                        sync.dma_start(
                            bass.AP(zl_scr, h * HD * T, [[CT, 8], [T, HD], [1, CT]]),
                            bass.AP(ph_sb, h * 8 * (HD * CT), [[HD * CT, 8], [CT, HD], [1, CT]])).then_inc(dB, 16)
                    for c in range(NCH):
                        sync.dma_start(
                            bass.AP(corrl_scr, c * 16 * D, [[1, 16384]]),
                            zro_sb[:]).then_inc(dB, 16)
                    sync.wait_ge(sV, eg_V)
                    sync.wait_ge(dB, b1_dB + 16 * (H + 2 * NCH))
                    for comp in range(2):
                        sync.dma_start(
                            bass.AP(corrl_scr, comp * 1024, [[2112, 16], [32768, 8], [1, 64]]),
                            bass.AP(eg_sb, comp * 64, [[1024, 16], [128, 8], [1, 64]])).then_inc(dB, 16)

                @blk.gpsimd
                def _(g):
                    g.wait_ge(sV, zloc_V)
                    for h in range(H // 2, H):
                        g.dma_start(
                            bass.AP(zl_scr, h * HD * T, [[CT, 8], [T, HD], [1, CT]]),
                            bass.AP(ph_sb, h * 8 * (HD * CT), [[HD * CT, 8], [CT, HD], [1, CT]])).then_inc(dB, 16)
                    for c in range(NCH, 2 * NCH):
                        g.dma_start(
                            bass.AP(corrl_scr, c * 16 * D, [[1, 16384]]),
                            zro_sb[:]).then_inc(dB, 16)

                ndB += 16 * (H + 2 * NCH + 2)

            _sb1.close()
            _sb0.close()

        zb_dB = ndB
        b_sV = nV

        with contextlib.ExitStack() as _sc_:
            EC = _sc_.enter_context
            x1_sb = EC(nc.sbuf_tensor("x1_sb", [128, OD * T], F32))
            mrow16 = EC(nc.sbuf_tensor("mrow16", [1, T], BF16))
            _sc2_ = contextlib.ExitStack()
            corr_sb = _sc2_.enter_context(nc.sbuf_tensor("corr_sb", [128, OD * T], BF16))
            wo_sb = _sc2_.enter_context(nc.sbuf_tensor("wo_sb", [128, KT * D], BF16))
            with contextlib.ExitStack() as _sd_:
                ED = _sd_.enter_context
                cl_sb = ED(nc.sbuf_tensor("cl_sb", [32, NCH * D], BF16))
                rr_sb = ED(nc.sbuf_tensor("rr_sb", [32, T], BF16))
                TB2, SB2 = nT, nS
                if True:
                    @blk.gpsimd
                    def _(g):
                        g.wait_ge(dB, zb_dB)
                        g.wait_ge(sV, b_sV)
                        g.dma_start(cl_sb[:], bass.AP(corrl_scr, 0,
                                    [[1024, 32], [32768, 8], [1, 1024]])).then_inc(dG, 16)
                        g.dma_start(rr_sb[:], bass.AP(rrow_scr, 0,
                                    [[T, 32], [1, T]])).then_inc(dG, 16)
                        for kt in range(KT):
                            g.dma_start(wo_sb[:, kt * D:(kt + 1) * D],
                                        woT[kt * 128:(kt + 1) * 128, :]).then_inc(dW, 16)

                    ndW += 16 * KT
                    wo_dW = ndW

                    ndB += 0
                    c0_dB = ndB

                    @blk.vector
                    def _(v):
                        v.wait_ge(dW, wo_dW)
                        for kt in range(KT):
                            v.tensor_reduce(rsumf[:, kt:kt + 1],
                                            wo_sb[:, kt * D:(kt + 1) * D], AX.X, ALU.add)
                        v.tensor_copy(bass.AP(rsum16, 0, [[2 * KT, 128], [2, KT]]),
                                      rsumf[:])
                        v.nop()
                        v.tensor_copy(bass.AP(rsum16, 0, [[2 * KT, 128], [2, KT]]),
                                      rsumf[:]).then_inc(sV, 1)
                        v.nop()

                    nV += 1
                    rs_V = nV

                    @blk.tensor
                    def _(t):
                        t.wait_ge(dG, ndG_a3 + 96)
                        i = 0
                        for od in range(OD):
                            for ch in range(NCH):
                                if i >= 2:
                                    t.wait_ge(sS, SB2 + i - 1)
                                t.matmul(pb[:, 3072 + (i % 2) * 512:3072 + (i % 2) * 512 + 256],
                                         cl_sb[:, ch * D + od * 128:ch * D + (od + 1) * 128],
                                         rr_sb[:, ch * CT:(ch + 1) * CT],
                                         start=True, stop=True).then_inc(sT, 1)
                                i += 1

                    nT += OD * NCH

                    @blk.scalar
                    def _(s):
                        i = 0
                        for od in range(OD):
                            for ch in range(NCH):
                                s.wait_ge(sT, TB2 + i + 1)
                                s.activation(corr_sb[:, od * T + ch * CT:od * T + (ch + 1) * CT],
                                             pb[:, 3072 + (i % 2) * 512:3072 + (i % 2) * 512 + 256],
                                             AF.Copy).then_inc(sS, 1)
                                i += 1

                    nS += OD * NCH

            with contextlib.ExitStack() as _se_:
                EE = _se_.enter_context
                zl_t = EE(nc.sbuf_tensor("zl_t", [128, 4 * 512], F32))
                qgc_t = EE(nc.sbuf_tensor("qgc_t", [128, 4 * 512], BF16))
                xr_t = EE(nc.sbuf_tensor("xr_t", [128, 2 * OD * 512], F32))
                og_t = EE(nc.sbuf_tensor("og_t", [128, 2 * OD * 512], BF16))
                at_t = EE(nc.sbuf_tensor("at_t", [128, OD * 512], BF16))
                sq_t = EE(nc.sbuf_tensor("sq_t", [128, 2 * 512], BF16))
                r2_sb = EE(nc.sbuf_tensor("r2_sb", [128, 512], F32))
                srow_a = EE(nc.sbuf_tensor("srow_a2", [1, 512], F32))[0:1, :]
                srow_c = EE(nc.sbuf_tensor("srow_c2", [1, 512], F32))[0:1, :]
                TC0, VC0, SC0 = nT, nV, nS
                VPT, TPT, SPT = 18, 18, 17

                if True:
                    ogod = {}; negd = {}; r2d = {}; appd = {}
                    _cv = [VC0]

                    def _inc():
                        _cv[0] += 1
                        return _cv[0]

                    @blk.vector
                    def _(v):
                        v.wait_ge(sS, SB2 + OD * NCH)

                        def og_block(tc):
                            off = (tc % 2) * OD * 512
                            if tc >= 2:
                                v.wait_ge(sT, TC0 + (tc - 2) * TPT + 9)
                            for od in range(OD):
                                nz = tc * 2 + od // 4 + 1
                                v.wait_ge([dCa, dCb, dCd, dCe][od % 4], 32 * nz)
                                v.tensor_add(zl_t[:, (od % 4) * 512:(od % 4 + 1) * 512],
                                             zl_t[:, (od % 4) * 512:(od % 4 + 1) * 512],
                                             corr_sb[:, od * T + tc * 512:od * T + (tc + 1) * 512])
                                v.tensor_mul(og_t[:, off + od * 512:off + (od + 1) * 512],
                                             zl_t[:, (od % 4) * 512:(od % 4 + 1) * 512],
                                             qgc_t[:, (od % 4) * 512:(od % 4 + 1) * 512]).then_inc(sV, 1)
                                ogod[(tc, od)] = _inc()

                        og_block(0)
                        for tc in range(TC):
                            base_t = TC0 + tc * TPT
                            v.wait_ge(sT, base_t + 1)
                            v.tensor_scalar(mrow16[0:1, tc * 512:(tc + 1) * 512],
                                            pb[0:1, 2560:3072], -1.0 / D, None, ALU.mult).then_inc(sV, 1)
                            negd[tc] = _inc()
                            if tc + 1 < TC:
                                og_block(tc + 1)
                            v.wait_ge(sT, base_t + TPT)
                            v.tensor_copy(r2_sb[:], pb[:, 3072:3584]).then_inc(sV, 1)
                            r2d[tc] = _inc()
                            v.wait_ge(dCc, 16 * OD * (tc + 1))
                            for od in range(OD):
                                v.tensor_mul(x1_sb[:, od * T + tc * 512:od * T + (tc + 1) * 512],
                                             at_t[:, od * 512:(od + 1) * 512], r2_sb[:])
                                v.tensor_add(x1_sb[:, od * T + tc * 512:od * T + (tc + 1) * 512],
                                             x1_sb[:, od * T + tc * 512:od * T + (tc + 1) * 512],
                                             xr_t[:, (tc % 2) * OD * 512 + od * 512:(tc % 2) * OD * 512 + (od + 1) * 512]).then_inc(sV, 1)
                                appd[(tc, od)] = _inc()

                    nV = _cv[0]

                    @blk.sync
                    def _(sync):
                        sync.wait_ge(sT, TB2 + OD * NCH)
                        sync.wait_ge(dG, ndG_a3 + 96)
                        csem = [dCa, dCb, dCd, dCe]
                        for tc in range(TC):
                            for od in range(OD):
                                if tc * OD + od >= 4:
                                    prev = tc * OD + od - 4
                                    ptc, pod = prev // OD, prev % OD
                                    sync.wait_ge(sV, ogod[(ptc, pod)])
                                sync.dma_start(
                                    zl_t[:, (od % 4) * 512:(od % 4 + 1) * 512],
                                    bass.AP(zl_scr, od * 128 * T + tc * 512,
                                            [[T, 128], [1, 512]])).then_inc(csem[od % 4], 16)
                                sync.dma_start(
                                    qgc_t[:, (od % 4) * 512:(od % 4 + 1) * 512],
                                    bass.AP(qg_scr, od * 128 * T + tc * 512,
                                            [[T, 128], [1, 512]])).then_inc(csem[od % 4], 16)
                            if tc >= 2:
                                sync.wait_ge(sV, appd[(tc - 2, OD - 1)])
                            for od in range(OD):
                                sync.dma_start(
                                    xr_t[:, (tc % 2) * OD * 512 + od * 512:(tc % 2) * OD * 512 + (od + 1) * 512],
                                    xT[od * 128:(od + 1) * 128, tc * 512:(tc + 1) * 512]).then_inc(dCc, 16)

                    @blk.tensor
                    def _(t):
                        t.wait_ge(sV, rs_V)
                        for tc in range(TC):
                            ogo = (tc % 2) * OD * 512
                            base_s = SC0 + tc * SPT
                            t.wait_ge(sV, ogod[(tc, OD - 1)])
                            for kt in range(KT):
                                mm = t.matmul(pb[0:1, 2560:3072], rsum16[:, 2 * kt:2 * kt + 1],
                                              og_t[:, ogo + kt * 512:ogo + (kt + 1) * 512],
                                              start=(kt == 0), stop=(kt == KT - 1))
                            mm.then_inc(sT, 1)
                            for op_ in range(4):
                                if op_ >= 2:
                                    t.wait_ge(sS, base_s + (op_ - 2) * 2 + 2)
                                elif tc >= 1:
                                    t.wait_ge(sS, SC0 + (tc - 1) * SPT + (op_ + 2) * 2 + 2)
                                for kt in range(KT):
                                    for j in range(2):
                                        od = 2 * op_ + j
                                        bslot = (op_ % 2) * 2 + j
                                        t.matmul(pb[:, bslot * 512:(bslot + 1) * 512],
                                                 wo_sb[:, kt * D + od * 128:kt * D + od * 128 + 128],
                                                 og_t[:, ogo + kt * 512:ogo + (kt + 1) * 512],
                                                 start=(kt == 0), stop=False)
                                t.wait_ge(sV, negd[tc])
                                for j in range(2):
                                    bslot = (op_ % 2) * 2 + j
                                    t.matmul(pb[:, bslot * 512:(bslot + 1) * 512],
                                             ones_1r16[:], mrow16[0:1, tc * 512:(tc + 1) * 512],
                                             start=False, stop=True).then_inc(sT, 1)
                            for od in range(OD):
                                t.wait_ge(sS, base_s + 8 + od + 1)
                                t.matmul(pb[0:1, 2048:2560], ones_p1[:],
                                         sq_t[:, (od % 2) * 512:(od % 2 + 1) * 512],
                                         start=(od == 0), stop=(od == OD - 1)).then_inc(sT, 1)
                            t.wait_ge(sS, base_s + SPT)
                            t.matmul(pb[:, 3072:3584], ones_1r[:],
                                     srow_c[:, 0:512],
                                     start=True, stop=True).then_inc(sT, 1)

                    nT += TC * TPT

                    @blk.scalar
                    def _(s):
                        for tc in range(TC):
                            base_t = TC0 + tc * TPT
                            for op_ in range(4):
                                for j in range(2):
                                    od = 2 * op_ + j
                                    bslot = (op_ % 2) * 2 + j
                                    s.wait_ge(sT, base_t + 1 + op_ * 2 + j + 1)
                                    if tc >= 1:
                                        s.wait_ge(sV, appd[(tc - 1, od)])
                                    s.activation(at_t[:, od * 512:(od + 1) * 512],
                                                 pb[:, bslot * 512:(bslot + 1) * 512],
                                                 AF.Copy).then_inc(sS, 1)
                            for od in range(OD):
                                if od >= 2:
                                    s.wait_ge(sT, base_t + 1 + 8 + od - 1)
                                s.activation(sq_t[:, (od % 2) * 512:(od % 2 + 1) * 512],
                                             at_t[:, od * 512:(od + 1) * 512],
                                             AF.Square).then_inc(sS, 1)
                            s.wait_ge(sT, base_t + 1 + 8 + 8)
                            s.activation(srow_a[:, 0:512], pb[0:1, 2048:2560],
                                         AF.Copy, scale=cst_sb[0:1, 2:3])
                            s.activation(srow_c[:, 0:512],
                                         srow_a[:, 0:512], AF.Ln, bias=cst_sb[0:1, 0:1])
                            s.activation(srow_c[:, 0:512],
                                         srow_c[:, 0:512], AF.Exp,
                                         scale=cst_sb[0:1, 1:2]).then_inc(sS, 1)

                    nS += TC * SPT

            c_sV, c_sS, c_sT = nV, nS, nT
            _sc2_.close()

            with contextlib.ExitStack() as _sf_:
                EF = _sf_.enter_context
                x1l_sb = EF(nc.sbuf_tensor("x1l_sb", [128, KT * T], BF16))
                rb_sb = EF(nc.sbuf_tensor("rb2_sb", [128, T], F32))
                negs16 = EF(nc.sbuf_tensor("negs16d", [1, T], BF16))
                cs_f = EF(nc.sbuf_tensor("cs_f", [1, D], BF16))
                wf_sb = EF(nc.sbuf_tensor("wf_sb", [128, KT * D], BF16))
                sf_sb = wf_sb
                y_t = EF(nc.sbuf_tensor("y_t", [128, OD * 512], BF16))
                sq2_t = EF(nc.sbuf_tensor("sq2_t", [128, 2 * 512], BF16))
                xsq2 = EF(nc.sbuf_tensor("xsq2", [128, T], BF16))
                ot_t = EF(nc.sbuf_tensor("ot_t", [128, 4 * 512], F32))
                r4_sb = rb_sb[:, 0:512]
                srow_a = EF(nc.sbuf_tensor("srow_a3", [1, T], F32))[0:1, :]
                srow_b = EF(nc.sbuf_tensor("srow_b3", [1, T], F32))[0:1, :]
                srow_c = EF(nc.sbuf_tensor("srow_c3", [1, T], F32))[0:1, :]
                stg_sb = EF(nc.sbuf_tensor("wfstg", [128, 2 * D], F32))
                TD0, VD0, SD0 = nT, nV, nS
                if True:
                    @blk.sync
                    def _(sync):
                        sync.wait_ge(sT, TD0)
                        sync.wait_ge(sV, c_sV)
                        sync.wait_ge(sS, c_sS)
                        for kt in range(KT):
                            if kt >= 2:
                                sync.wait_ge(sS, SD0 + kt - 1)
                            sync.dma_start(stg_sb[:, (kt % 2) * D:(kt % 2 + 1) * D],
                                           wfT[kt * 128:(kt + 1) * 128, :]
                                           ).then_inc(wA if kt % 2 == 0 else wB, 16)

                    @blk.scalar
                    def _(s):
                        for kt in range(KT):
                            s.wait_ge(wA if kt % 2 == 0 else wB,
                                      (nwA if kt % 2 == 0 else nwB) + 16 * (kt // 2 + 1))
                            s.sign(sf_sb[:, kt * D:(kt + 1) * D],
                                   stg_sb[:, (kt % 2) * D:(kt % 2 + 1) * D]).then_inc(sS, 1)
                        s.wait_ge(sV, c_sV)
                        for kt in range(KT):
                            if kt >= 1:
                                s.wait_ge(sT, TD0 + kt)
                            s.activation(xsq2[:, 0:T],
                                         x1_sb[:, kt * T:(kt + 1) * T],
                                         AF.Square).then_inc(sS, 1)

                    nS += 2 * KT
                    sgn_S = SD0 + KT

                    @blk.vector
                    def _(v):
                        v.wait_ge(sS, sgn_S)
                        for kt in range(KT):
                            v.tensor_reduce(rsumf[:, kt:kt + 1],
                                            sf_sb[:, kt * D:(kt + 1) * D], AX.X, ALU.add)
                        v.tensor_copy(bass.AP(rsum16, 0, [[2 * KT, 128], [2, KT]]),
                                      rsumf[:])
                        v.nop()
                        v.tensor_copy(bass.AP(rsum16, 0, [[2 * KT, 128], [2, KT]]),
                                      rsumf[:]).then_inc(sV, 1)
                        v.nop()

                    nV += 1
                    rs2_V = nV

                    @blk.tensor
                    def _(t):
                        t.wait_ge(sV, c_sV)
                        for kt in range(KT):
                            t.wait_ge(sS, sgn_S + kt + 1)
                            for tc in range(TC):
                                t.matmul(pb[0:1, tc * 512:(tc + 1) * 512], ones_p1f[:],
                                         x1_sb[:, kt * T + tc * 512:kt * T + (tc + 1) * 512],
                                         start=(kt == 0), stop=(kt == KT - 1))
                            for tc in range(TC):
                                mm = t.matmul(pb[0:1, 2048 + tc * 512:2048 + (tc + 1) * 512],
                                              ones_p1[:],
                                              xsq2[:, tc * 512:(tc + 1) * 512],
                                              start=(kt == 0), stop=(kt == KT - 1))
                            mm.then_inc(sT, 1)

                    nT += KT
                    d0_T = nT

                    @blk.vector
                    def _(v):
                        v.wait_ge(sT, d0_T)
                        v.tensor_scalar(srow_a, pb[0:1, 0:2048], 1.0 / D, None, ALU.mult)
                        v.tensor_scalar(srow_b, pb[0:1, 2048:4096], 1.0 / D, None, ALU.mult)
                        v.tensor_mul(srow_c, srow_a, srow_a)
                        v.tensor_sub(srow_b, srow_b, srow_c).then_inc(sV, 1)

                    nV += 1
                    d0_V = nV

                    @blk.scalar
                    def _(s):
                        s.wait_ge(sV, d0_V)
                        s.activation(srow_c, srow_b, AF.Ln, bias=cst_sb[0:1, 0:1])
                        s.activation(srow_c, srow_c, AF.Exp, scale=cst_sb[0:1, 1:2]).then_inc(sS, 1)

                    nS += 1
                    d0_S = nS

                    @blk.vector
                    def _(v):
                        v.wait_ge(sS, d0_S)
                        v.scalar_tensor_tensor(negs16[:], srow_a, -1.0, srow_c,
                                               ALU.mult, ALU.mult).then_inc(sV, 1)

                    nV += 1
                    d0_V2 = nV

                    @blk.tensor
                    def _(t):
                        t.wait_ge(sV, d0_V2)
                        for tc in range(TC):
                            t.matmul(pb[:, 2048 + tc * 512:2048 + (tc + 1) * 512],
                                     ones_1r[:], srow_c[:, tc * 512:(tc + 1) * 512],
                                     start=True, stop=True)
                        for kt in range(KT):
                            for nh in range(2):
                                mm = t.matmul(pb[0:1, nh * 512:(nh + 1) * 512], ones_p1[:],
                                              sf_sb[:, kt * D + nh * 512:kt * D + (nh + 1) * 512],
                                              start=(kt == 0), stop=(kt == KT - 1))
                        mm.then_inc(sT, 1)

                    nT += 1
                    d0_T2 = nT

                    @blk.vector
                    def _(v):
                        v.wait_ge(sT, d0_T2)
                        v.tensor_copy(cs_f[:], pb[0:1, 0:1024])
                        for tc in range(TC):
                            v.tensor_copy(rb_sb[:, tc * 512:(tc + 1) * 512],
                                          pb[:, 2048 + tc * 512:2048 + (tc + 1) * 512])
                        for od in range(OD):
                            vv = v.tensor_mul(x1l_sb[:, od * T:(od + 1) * T],
                                              x1_sb[:, od * T:(od + 1) * T], rb_sb[:])
                        vv.then_inc(sV, 1)

                    nV += 1
                    x1l_V = nV

                TE0, VE0, SE0, OE0 = nT, nV, nS, ndO
                TPE, VPE, SPE = 18, 10, 17
                if True:
                    @blk.tensor
                    def _(t):
                        t.wait_ge(sV, x1l_V)
                        t.wait_ge(sV, rs2_V)
                        for tc in range(TC):
                            base_s = SE0 + tc * SPE
                            base_v = VE0 + tc * VPE
                            if tc >= 1:
                                t.wait_ge(sV, VE0 + (tc - 1) * VPE + 2)
                            for kt in range(KT):
                                mm = t.matmul(pb[0:1, 2560:3072], rsum16[:, 2 * kt:2 * kt + 1],
                                              x1l_sb[:, kt * T + tc * 512:kt * T + (tc + 1) * 512],
                                              start=(kt == 0), stop=(kt == KT - 1))
                            mm.then_inc(sT, 1)
                            for op_ in range(4):
                                if op_ >= 2:
                                    t.wait_ge(sS, base_s + (op_ - 2) * 2 + 2)
                                elif tc >= 1:
                                    t.wait_ge(sS, SE0 + (tc - 1) * SPE + (op_ + 2) * 2 + 2)
                                for kt in range(KT):
                                    for j in range(2):
                                        od = 2 * op_ + j
                                        bslot = (op_ % 2) * 2 + j
                                        t.matmul(pb[:, bslot * 512:(bslot + 1) * 512],
                                                 sf_sb[:, kt * D + od * 128:kt * D + od * 128 + 128],
                                                 x1l_sb[:, kt * T + tc * 512:kt * T + (tc + 1) * 512],
                                                 start=(kt == 0), stop=False)
                                t.wait_ge(sV, base_v + 1)
                                for j in range(2):
                                    od = 2 * op_ + j
                                    bslot = (op_ % 2) * 2 + j
                                    t.matmul(pb[:, bslot * 512:(bslot + 1) * 512],
                                             cs_f[0:1, od * 128:(od + 1) * 128],
                                             negs16[0:1, tc * 512:(tc + 1) * 512],
                                             start=False, stop=False)
                                    t.matmul(pb[:, bslot * 512:(bslot + 1) * 512],
                                             ones_1r16[:], mrow16[0:1, tc * 512:(tc + 1) * 512],
                                             start=False, stop=True).then_inc(sT, 1)
                            for od in range(OD):
                                t.wait_ge(sS, base_s + 8 + od + 1)
                                t.matmul(pb[0:1, 2048:2560], ones_p1[:],
                                         sq2_t[:, (od % 2) * 512:(od % 2 + 1) * 512],
                                         start=(od == 0), stop=(od == OD - 1)).then_inc(sT, 1)
                            t.wait_ge(sS, base_s + SPE)
                            t.matmul(pb[:, 3072:3584], ones_1r[:],
                                     srow_c[:, tc * 512:(tc + 1) * 512],
                                     start=True, stop=True).then_inc(sT, 1)

                    nT += TC * TPE

                    @blk.scalar
                    def _(s):
                        for tc in range(TC):
                            base_t = TE0 + tc * TPE
                            for op_ in range(4):
                                for j in range(2):
                                    od = 2 * op_ + j
                                    bslot = (op_ % 2) * 2 + j
                                    s.wait_ge(sT, base_t + 1 + op_ * 2 + j + 1)
                                    if tc >= 1:
                                        s.wait_ge(sV, VE0 + (tc - 1) * VPE + 2 + od + 1)
                                    s.activation(y_t[:, od * 512:(od + 1) * 512],
                                                 pb[:, bslot * 512:(bslot + 1) * 512],
                                                 AF.Copy).then_inc(sS, 1)
                            for od in range(OD):
                                if od >= 2:
                                    s.wait_ge(sT, base_t + 1 + 8 + od - 1)
                                s.activation(sq2_t[:, (od % 2) * 512:(od % 2 + 1) * 512],
                                             y_t[:, od * 512:(od + 1) * 512],
                                             AF.Square).then_inc(sS, 1)
                            s.wait_ge(sT, base_t + 1 + 8 + 8)
                            s.activation(srow_a[:, tc * 512:(tc + 1) * 512], pb[0:1, 2048:2560], AF.Copy, scale=cst_sb[0:1, 2:3])
                            s.activation(srow_c[:, tc * 512:(tc + 1) * 512],
                                         srow_a[:, tc * 512:(tc + 1) * 512], AF.Ln, bias=cst_sb[0:1, 0:1])
                            s.activation(srow_c[:, tc * 512:(tc + 1) * 512],
                                         srow_c[:, tc * 512:(tc + 1) * 512], AF.Exp,
                                         scale=cst_sb[0:1, 1:2]).then_inc(sS, 1)

                    nS += TC * SPE

                    @blk.vector
                    def _(v):
                        for tc in range(TC):
                            base_t = TE0 + tc * TPE
                            v.wait_ge(sT, base_t + 1)
                            v.tensor_scalar(mrow16[0:1, tc * 512:(tc + 1) * 512],
                                            pb[0:1, 2560:3072], -1.0 / D, None, ALU.mult).then_inc(sV, 1)
                            v.wait_ge(sT, base_t + TPE)
                            v.tensor_copy(r4_sb, pb[:, 3072:3584]).then_inc(sV, 1)
                            for od in range(OD):
                                if tc * OD + od >= 4:
                                    v.wait_ge(dO, OE0 + 16 * (tc * OD + od - 3))
                                v.tensor_mul(ot_t[:, (od % 4) * 512:(od % 4 + 1) * 512],
                                             y_t[:, od * 512:(od + 1) * 512], r4_sb)
                                v.tensor_add(ot_t[:, (od % 4) * 512:(od % 4 + 1) * 512],
                                             ot_t[:, (od % 4) * 512:(od % 4 + 1) * 512],
                                             x1_sb[:, od * T + tc * 512:od * T + (tc + 1) * 512]).then_inc(sV, 1)

                    nV += TC * VPE

                    @blk.sync
                    def _(sync):
                        for tc in range(TC):
                            base_v = VE0 + tc * VPE
                            for od in range(OD):
                                sync.wait_ge(sV, base_v + 2 + od + 1)
                                sync.dma_start(
                                    out[od * 128:(od + 1) * 128, tc * 512:(tc + 1) * 512],
                                    ot_t[:, (od % 4) * 512:(od % 4 + 1) * 512]).then_inc(dO, 16)
                        sync.wait_ge(dO, OE0 + 16 * TC * OD)

                    ndO += 16 * TC * OD

    return nc


def kernel(L0_attn_x, L0_attn_Wq, L0_attn_Wk, L0_attn_Wv, L0_attn_Wo, L0_attn_Wb,
           L0_attn_Wfd, L0_attn_bfd, L0_attn_Wsd, L0_attn_bsd, L0_attn_Wg,
           L0_ffn_weight):
    x = np.asarray(L0_attn_x, np.float32)
    B, S, Dm = x.shape
    xT = np.ascontiguousarray(x.reshape(B * S, Dm).T)

    wT = {
        "wqT": np.ascontiguousarray(np.asarray(L0_attn_Wq, np.float32).T),
        "wkT": np.ascontiguousarray(np.asarray(L0_attn_Wk, np.float32).T),
        "wvT": np.ascontiguousarray(np.asarray(L0_attn_Wv, np.float32).T),
        "wgT": np.ascontiguousarray(np.asarray(L0_attn_Wg, np.float32).T),
        "woT": np.ascontiguousarray(np.asarray(L0_attn_Wo, np.float32).T),
        "wfT": np.ascontiguousarray(np.asarray(L0_ffn_weight, np.float32).T),
    }
    wbfs = np.concatenate([np.asarray(L0_attn_Wb, np.float32),
                           np.asarray(L0_attn_Wfd, np.float32),
                           np.asarray(L0_attn_Wsd, np.float32)], axis=0)
    wbfsT = np.ascontiguousarray(wbfs.T)
    bias48 = np.ascontiguousarray(np.concatenate(
        [np.zeros(16, np.float32),
         np.asarray(L0_attn_bfd, np.float32),
         np.asarray(L0_attn_bsd, np.float32)]).astype(np.float32)[:, None])

    ib16 = np.zeros((16, D), ml_dtypes.bfloat16)
    for hh in range(16):
        ib16[hh, hh * 64:(hh + 1) * 64] = 1.0
    cst = np.array([[EPS, -0.5, 1.0 / D, 0.0]], np.float32)

    nc = build_nc()
    in_maps = []
    for c in range(NCORE):
        sel = np.zeros((16, 8), np.float32)
        sel[:, c] = 1.0
        m = {"xT": np.ascontiguousarray(xT[:, c * T:(c + 1) * T]),
             "wbfsT": wbfsT, "bias48": bias48, "ib16": ib16, "sel16": sel,
             "cst": cst}
        m.update(wT)
        in_maps.append(m)

    res = run_bass_kernel_spmd(nc, in_maps, core_ids=list(range(NCORE)))
    outT = np.concatenate([res.results[c]["out"] for c in range(NCORE)], axis=1)
    return np.ascontiguousarray(outT.T).reshape(B, S, Dm)


if __name__ == "__main__":
    build_nc()
    print("built ok")
